# version 09 (final)
# baseline (speedup 1.0000x reference)
import numpy as np
from collections import defaultdict
from contextlib import ExitStack

import concourse.bass as bass
import concourse.mybir as mybir
from concourse.bass_utils import run_bass_kernel_spmd

F32 = mybir.dt.float32
BF16 = mybir.dt.bfloat16
AF = mybir.ActivationFunctionType
ALU = mybir.AluOpType
AX = mybir.AxisListType

NCORES = 8
D = 4096
DT = 32
T = 2048
HH = 64
PP = 64
NST = 128
GG = 8
POT = 81
MoK, MoTOP, MoR = 64, 16, 16
EPS = 1e-5

ENGS = ("t", "v", "a", "g", "s")
SAME_ENGINE_WAITS = True


def cdiv(a, b):
    return (a + b - 1) // b


def pad128(n):
    return cdiv(n, 128) * 128


def chunks_of(total, w):
    out = []
    c = 0
    while c < total:
        out.append([c, min(w, total - c)])
        c += w
    if len(out) > 1 and out[-1][1] == 1:
        out[-2][1] -= 1
        out[-1] = [out[-1][0] - 1, 2]
    return [tuple(x) for x in out]


def _buf_key(ap):
    return ap.tensor.name


class Kit:
    DMA_POOL = 12

    def __init__(self, nc, ctx):
        self.nc = nc
        self.eng = {"t": nc.tensor, "v": nc.vector, "a": nc.scalar,
                    "g": nc.gpsimd, "s": nc.sync}
        self.esem = {e: ctx.enter_context(nc.semaphore(f"k_e_{e}")) for e in ENGS}
        self.ecnt = {e: 0 for e in ENGS}
        self.dsem = {q: [ctx.enter_context(nc.semaphore(f"k_d_{q}{i}"))
                         for i in range(self.DMA_POOL)] for q in ("s", "a", "g")}
        self.dcnt = {q: [0] * self.DMA_POOL for q in ("s", "a", "g")}
        self.dgen = {q: 0 for q in ("s", "a", "g")}
        self.water = defaultdict(lambda: defaultdict(int))
        self.writer = {}
        self.readers = defaultdict(list)

    def _wait(self, e, tok):
        if tok is None:
            return
        sem, val = tok
        if (not SAME_ENGINE_WAITS and e in ("v",)
                and sem is self.esem.get(e)):
            return
        if self.water[e][sem.name] >= val:
            return
        self.eng[e].wait_ge(sem, val)
        self.water[e][sem.name] = val

    def _deps_for(self, reads, writes):
        toks = []
        for ap in reads:
            t_ = self.writer.get(_buf_key(ap))
            if t_ is not None:
                toks.append(t_)
        for ap in writes:
            kk = _buf_key(ap)
            t_ = self.writer.get(kk)
            if t_ is not None:
                toks.append(t_)
            toks.extend(self.readers.get(kk, ()))
        return toks

    def _record(self, reads, writes, tok):
        for ap in reads:
            self.readers[_buf_key(ap)].append(tok)
        for ap in writes:
            kk = _buf_key(ap)
            self.writer[kk] = tok
            self.readers[kk] = []

    def op(self, e, meth, *args, reads=(), writes=(), deps=(), **kw):
        for tok in self._deps_for(reads, writes):
            self._wait(e, tok)
        for tok in deps:
            self._wait(e, tok)
        ins = getattr(self.eng[e], meth)(*args, **kw)
        sem = self.esem[e]
        self.ecnt[e] += 1
        ins.then_inc(sem, 1)
        tok = (sem, self.ecnt[e])
        self._record(reads, writes, tok)
        return tok

    def mm_group(self):
        return _MMGroup(self, "t")

    def dma(self, q, out, in_, reads=(), writes=(), deps=(), **kw):
        reads = list(reads) or [in_]
        writes = list(writes) or [out]
        for tok in self._deps_for(reads, writes):
            self._wait(q, tok)
        for tok in deps:
            self._wait(q, tok)
        slot = self.dgen[q] % self.DMA_POOL
        self.dgen[q] += 1
        sem = self.dsem[q][slot]
        prev = self.dcnt[q][slot]
        if prev:
            self._wait(q, (sem, prev))
        ins = self.eng[q].dma_start(out=out, in_=in_, **kw)
        ins.then_inc(sem, 16)
        self.dcnt[q][slot] = prev + 16
        tok = (sem, prev + 16)
        self._record(reads, writes, tok)
        return tok

    def barrier_tokens(self):
        toks = [(self.esem[e], self.ecnt[e]) for e in ENGS if self.ecnt[e]]
        for q in ("s", "a", "g"):
            for i in range(self.DMA_POOL):
                if self.dcnt[q][i]:
                    toks.append((self.dsem[q][i], self.dcnt[q][i]))
        return toks

    def phase_barrier(self):
        toks = self.barrier_tokens()
        for e in ENGS:
            for tok in toks:
                self._wait(e, tok)
        self.writer = {}
        self.readers = defaultdict(list)


class _MMGroup:
    def __init__(self, kit, e):
        self.kit = kit
        self.e = e
        self.first = True
        self.pr = []
        self.pw = []

    def mm(self, out, lhsT, rhs, last, reads=(), writes=(), deps=(), inc_now=False,
           single=False, **kw):
        kit = self.kit
        reads = list(reads) or [lhsT, rhs]
        writes = list(writes) or [out]
        for tok in kit._deps_for(reads, writes):
            kit._wait(self.e, tok)
        for tok in deps:
            kit._wait(self.e, tok)
        st_ = True if single else self.first
        sp_ = True if single else last
        ins = kit.eng[self.e].matmul(out, lhsT, rhs, start=st_, stop=sp_, **kw)
        self.first = False
        if inc_now and not last:
            sem = kit.esem[self.e]
            kit.ecnt[self.e] += 1
            ins.then_inc(sem, 1)
            tok = (sem, kit.ecnt[self.e])
            for ap in reads:
                kit.readers[_buf_key(ap)].append(tok)
            self.pw += writes
            return None
        self.pr += reads
        self.pw += writes
        if last:
            sem = kit.esem[self.e]
            kit.ecnt[self.e] += 1
            ins.then_inc(sem, 1)
            tok = (sem, kit.ecnt[self.e])
            kit._record(self.pr, self.pw, tok)
            self.pr, self.pw = [], []
            self.first = True
            return tok
        return None



class Consts:
    def __init__(self, nc, ctx, k):
        S = lambda n, s, d: ctx.enter_context(nc.sbuf_tensor(n, s, d))
        dr = lambda n, s, d: nc.dram_tensor(n, s, d, kind="ExternalInput").ap()
        self.d = {
            "id128f": dr("c_id128f", [128, 128], F32),
            "id64f": dr("c_id64f", [64, 64], F32),
            "onescol_b": dr("c_onescol_b", [128, 1], BF16),
            "ones1f": dr("c_ones1f", [1, 128], F32),
            "moexp": dr("c_moexp", [64, 8 * 128], BF16),
            "dtexp": dr("c_dtexp", [64, 32 * 128], BF16),
            "id128b": dr("c_id128b", [128, 128], BF16),
            "jrow": dr("c_jrow", [128, 128], F32),
        }
        self.id128f = S("id128f", [128, 128], F32)
        self.id64f = S("id64f", [64, 64], F32)
        self.onescol_b = S("onescol_b", [128, 1], BF16)
        self.ones1f = S("ones1f", [1, 128], F32)
        self.moexp = S("moexp", [64, 8, 128], BF16)
        self.dtexp = S("dtexp", [64, 32, 128], BF16)
        self.id128b = S("id128b", [128, 128], BF16)
        self.jrow = S("jrow", [128, 128], F32)
        k.dma("s", self.id128f[:], self.d["id128f"])
        k.dma("s", self.id64f[:], self.d["id64f"])
        k.dma("s", self.onescol_b[:], self.d["onescol_b"])
        k.dma("s", self.ones1f[:], self.d["ones1f"])
        k.dma("s", self.moexp[:], self.d["moexp"].rearrange("h (j m) -> h j m", m=128))
        k.dma("s", self.dtexp[:], self.d["dtexp"].rearrange("h (j m) -> h j m", m=128))
        k.dma("s", self.id128b[:], self.d["id128b"])
        k.dma("s", self.jrow[:], self.d["jrow"])

    @staticmethod
    def host_inputs():
        import ml_dtypes
        b = lambda x: np.ascontiguousarray(x).astype(ml_dtypes.bfloat16)
        moexp = np.zeros((64, 8 * 128), np.float32)
        for j in range(8):
            for m in range(128):
                moexp[8 * j + m // 16, j * 128 + m] = 1.0
        dtexp = np.zeros((64, 32 * 128), np.float32)
        for j in range(32):
            for m in range(128):
                dtexp[2 * j + m // 64, j * 128 + m] = 1.0
        return {
            "c_id128f": np.eye(128, dtype=np.float32),
            "c_id64f": np.eye(64, dtype=np.float32),
            "c_onescol_b": b(np.ones((128, 1), np.float32)),
            "c_ones1f": np.ones((1, 128), np.float32),
            "c_moexp": b(moexp),
            "c_dtexp": b(dtexp),
            "c_id128b": b(np.eye(128, dtype=np.float32)),
            "c_jrow": np.tile(np.arange(128, dtype=np.float32)[None, :], (128, 1)),
        }



def ph_input_tokmajor(k, nc, con, x_ext, ncols, xchan_hbm):
    ntt = cdiv(ncols, 128)
    with ExitStack() as c2:
        S = lambda n, s, d: c2.enter_context(nc.sbuf_tensor(n, s, d))
        PS = lambda n, s, d: c2.enter_context(nc.psum_tensor(n, s, d))
        xt = [S(f"pi_xt{i}", [128, D], F32) for i in range(2)]
        stg = [S(f"pi_st{i}", [128, 128 * DT], F32) for i in range(2)]
        ptr = [PS(f"pi_p{i}", [128, 128], F32) for i in range(2)]
        for tt in range(ntt):
            buf = tt % 2
            c0 = tt * 128
            w = min(128, ncols - c0)
            if w < 128:
                k.op("v", "memset", xt[buf][:], 0.0, writes=[xt[buf][:]])
            k.dma("s", xt[buf][:w, :], x_ext[c0:c0 + w, :])
            for dt_ in range(DT):
                pb = dt_ % 2
                k.op("t", "transpose", ptr[pb][:], xt[buf][:, dt_ * 128:(dt_ + 1) * 128],
                     con.id128f[:], reads=[xt[buf][:], con.id128f[:]], writes=[ptr[pb][:]])
                k.op("a", "copy", stg[buf][:, dt_ * 128:(dt_ + 1) * 128], ptr[pb][:],
                     reads=[ptr[pb][:]], writes=[stg[buf][:]])
            k.dma("a", xchan_hbm.rearrange("dt p t -> p dt t")[:, :, c0:c0 + 128],
                  stg[buf][:].rearrange("p (dt c) -> p dt c", c=128))
        k.phase_barrier()


def ph_output_tokmajor(k, nc, con, xchan_hbm, ncols, out_ext):
    ntt = cdiv(ncols, 128)
    with ExitStack() as c2:
        S = lambda n, s, d: c2.enter_context(nc.sbuf_tensor(n, s, d))
        PS = lambda n, s, d: c2.enter_context(nc.psum_tensor(n, s, d))
        xt = [S(f"po_xt{i}", [128, 128 * DT], F32) for i in range(2)]
        stg = [S(f"po_st{i}", [128, D], F32) for i in range(2)]
        ptr = [PS(f"po_p{i}", [128, 128], F32) for i in range(2)]
        for tt in range(ntt):
            buf = tt % 2
            c0 = tt * 128
            w = min(128, ncols - c0)
            k.dma("s", xt[buf][:].rearrange("p (dt c) -> p dt c", c=128),
                  xchan_hbm.rearrange("dt p t -> p dt t")[:, :, c0:c0 + 128])
            for dt_ in range(DT):
                pb = dt_ % 2
                k.op("t", "transpose", ptr[pb][:], xt[buf][:, dt_ * 128:(dt_ + 1) * 128],
                     con.id128f[:], reads=[xt[buf][:], con.id128f[:]], writes=[ptr[pb][:]])
                k.op("a", "copy", stg[buf][:, dt_ * 128:(dt_ + 1) * 128], ptr[pb][:],
                     reads=[ptr[pb][:]], writes=[stg[buf][:]])
            k.dma("a", out_ext[c0:c0 + w, :], stg[buf][:w, :])
        k.phase_barrier()



def ln_chan(k, nc, con, x_hbm, ncols, xn_sb, tag, cw=256):
    with ExitStack() as c2:
        S = lambda n, s, d: c2.enter_context(nc.sbuf_tensor(n, s, d))
        with ExitStack() as c2b:
            PS = lambda n, s, d: c2b.enter_context(nc.psum_tensor(n, s, d))
            xbuf = S(f"ln_x{tag}", [128, DT, cw], F32)
            xsq = S(f"ln_sq{tag}", [128, DT, cw], BF16)
            t1f = xsq
            psums = PS(f"ln_ps{tag}", [1, cw], F32)
            psumq = PS(f"ln_pq{tag}", [1, cw], F32)
            mrow = S(f"ln_mr{tag}", [1, cw], F32)
            qrow = S(f"ln_qr{tag}", [1, cw], F32)
            prow = S(f"ln_pr{tag}", [1, cw], F32)
            rrow = S(f"ln_rr{tag}", [1, cw], F32)
            pbc = PS(f"ln_bc{tag}", [128, cw], F32)
            bcm = S(f"ln_bm{tag}", [128, cw], F32)
            bcr = S(f"ln_br{tag}", [128, cw], F32)
            xhr = x_hbm.rearrange("dt p t -> p dt t")
            for ci, (c0, w) in enumerate(chunks_of(ncols, cw)):
                k.dma("s", xbuf[:, :, :w], xhr[:, :, c0:c0 + w])
                for dt_ in range(DT):
                    k.op("v", "tensor_copy", xn_sb[:, dt_, c0:c0 + w],
                         xbuf[:, dt_, :w], reads=[xbuf[:]], writes=[xn_sb[:]])
                k.op("a", "activation", xsq[:, :, :w],
                     xn_sb[:, :, c0:c0 + w], AF.Square,
                     reads=[xn_sb[:]], writes=[xsq[:]])
                gs = k.mm_group()
                gq = k.mm_group()
                for dt_ in range(DT):
                    gs.mm(psums[:, :w], con.onescol_b[:], xn_sb[:, dt_, c0:c0 + w],
                          last=(dt_ == DT - 1),
                          reads=[con.onescol_b[:], xn_sb[:]], writes=[psums[:]])
                for dt_ in range(DT):
                    gq.mm(psumq[:, :w], con.onescol_b[:], xsq[:, dt_, :w],
                          last=(dt_ == DT - 1),
                          reads=[con.onescol_b[:], xsq[:]], writes=[psumq[:]])
                k.op("v", "tensor_scalar_mul", mrow[:, :w], psums[:, :w], 1.0 / D,
                     reads=[psums[:]], writes=[mrow[:]])
                k.op("v", "tensor_scalar_mul", qrow[:, :w], psumq[:, :w], 1.0 / D,
                     reads=[psumq[:]], writes=[qrow[:]])
                k.op("v", "tensor_tensor", prow[:, :w], mrow[:, :w], mrow[:, :w],
                     ALU.mult, reads=[mrow[:]], writes=[prow[:]])
                k.op("v", "tensor_sub", qrow[:, :w], qrow[:, :w], prow[:, :w],
                     reads=[qrow[:], prow[:]], writes=[qrow[:]])
                k.op("v", "tensor_scalar_add", qrow[:, :w], qrow[:, :w], EPS,
                     reads=[qrow[:]], writes=[qrow[:]])
                k.op("a", "activation", qrow[:, :w], qrow[:, :w], AF.Ln,
                     reads=[qrow[:]], writes=[qrow[:]])
                k.op("a", "activation", rrow[:, :w], qrow[:, :w], AF.Exp, scale=-0.5,
                     reads=[qrow[:]], writes=[rrow[:]])
                gb = k.mm_group()
                gb.mm(pbc[:, :w], con.ones1f[:], mrow[:, :w], last=True,
                      reads=[con.ones1f[:], mrow[:]], writes=[pbc[:]])
                k.op("a", "copy", bcm[:, :w], pbc[:, :w], reads=[pbc[:]], writes=[bcm[:]])
                gb2 = k.mm_group()
                gb2.mm(pbc[:, :w], con.ones1f[:], rrow[:, :w], last=True,
                       reads=[con.ones1f[:], rrow[:]], writes=[pbc[:]])
                k.op("a", "copy", bcr[:, :w], pbc[:, :w], reads=[pbc[:]], writes=[bcr[:]])
                for dt_ in range(DT):
                    k.op("v", "tensor_sub", t1f[:, dt_, :w], xbuf[:, dt_, :w],
                         bcm[:, :w], reads=[xbuf[:], bcm[:]], writes=[t1f[:]])
                for dt_ in range(DT):
                    k.op("g", "tensor_mul", xn_sb[:, dt_, c0:c0 + w], t1f[:, dt_, :w],
                         bcr[:, :w], reads=[t1f[:], bcr[:]], writes=[xn_sb[:]])
        k.phase_barrier()



def ph_ffn1(k, nc, con, W, x_hbm, ncols, out_hbm, tag):
    with ExitStack() as c2:
        S = lambda n, s, d: c2.enter_context(nc.sbuf_tensor(n, s, d))
        xn = S(f"f1_xn{tag}", [128, DT, ncols], BF16)
        ln_chan(k, nc, con, x_hbm, ncols, xn, tag + "L")
        with ExitStack() as c3:
            S3 = lambda n, s, d: c3.enter_context(nc.sbuf_tensor(n, s, d))
            PS3 = lambda n, s, d: c3.enter_context(nc.psum_tensor(n, s, d))
            wsl = [S3(f"f1_w{tag}{i}", [128, DT, 128], BF16) for i in range(2)]
            bias = S3(f"f1_b{tag}", [128, DT], F32)
            sclcol = S3(f"f1_s{tag}", [128, 1], F32)
            scl = S3(f"f1_s1{tag}", [1, 1], F32)
            pso = [PS3(f"f1_p{tag}{i}", [128, 512], F32) for i in range(8)]
            tmp = [S3(f"f1_t{tag}{i}", [128, 512], F32) for i in range(2)]
            xold = [S3(f"f1_xo{tag}{i}", [128, 512], F32) for i in range(2)]
            xnew = [S3(f"f1_xw{tag}{i}", [128, 512], F32) for i in range(2)]
            k.dma("s", bias[:], W["bias"])
            k.op("v", "tensor_scalar_mul", bias[:], bias[:], 0.5,
                 reads=[bias[:]], writes=[bias[:]])
            k.dma("s", scl[:], W["scale"].unsqueeze(0))
            gb = k.mm_group()
            gb.mm(pso[0][:, 0:1], con.ones1f[:], scl[:], last=True,
                  reads=[con.ones1f[:], scl[:]], writes=[pso[0][:]])
            k.op("a", "copy", sclcol[:], pso[0][:, 0:1], reads=[pso[0][:]], writes=[sclcol[:]])
            k.op("v", "tensor_scalar_mul", sclcol[:], sclcol[:], 0.5,
                 reads=[sclcol[:]], writes=[sclcol[:]])
            xhr = x_hbm.rearrange("dt p t -> p dt t")
            ohr = out_hbm.rearrange("dt p t -> p dt t")
            cks = chunks_of(ncols, 512)
            for ot in range(DT):
                buf = ot % 2
                k.dma("g", wsl[buf][:], W["w"][ot].rearrange("kt kk m -> kk kt m"))
                grp = [k.mm_group() for _ in cks]
                for kt in range(DT):
                    for ci, (c0, w) in enumerate(cks):
                        bank = ((ot % 2) * 4 + ci) % 8
                        grp[ci].mm(pso[bank][:, :w], wsl[buf][:, kt, :],
                                   xn[:, kt, c0:c0 + w], last=(kt == DT - 1),
                                   reads=[wsl[buf][:], xn[:]], writes=[pso[bank][:]])
                for ci, (c0, w) in enumerate(cks):
                    bank = ((ot % 2) * 4 + ci) % 8
                    cb = ci % 2
                    k.dma("s", xold[cb][:, :w], xhr[:, ot, c0:c0 + w])
                    k.op("v", "tensor_scalar", tmp[cb][:, :w], pso[bank][:, :w],
                         sclcol[:], bias[:, ot:ot + 1], ALU.mult, ALU.add,
                         reads=[pso[bank][:], bias[:], sclcol[:]], writes=[tmp[cb][:]])
                    k.op("g", "tensor_add", xnew[cb][:, :w], tmp[cb][:, :w],
                         xold[cb][:, :w],
                         reads=[tmp[cb][:], xold[cb][:]], writes=[xnew[cb][:]])
                    k.dma("a", ohr[:, ot, c0:c0 + w], xnew[cb][:, :w])
            k.phase_barrier()



def _tiles_lhsT(wT, kdim, mdim):
    mt = cdiv(mdim, 128)
    kt = kdim // 128
    wpad = np.zeros((kdim, mt * 128), np.float32)
    wpad[:, :mdim] = wT
    return np.ascontiguousarray(wpad.reshape(kt, 128, mt, 128).transpose(2, 0, 1, 3))


def chan_pd(vec):
    return np.ascontiguousarray(np.asarray(vec, np.float32).reshape(DT, 128).T)


def host_prep_ffn1(weight, weight_scale, bias):
    wT = np.ascontiguousarray(np.asarray(weight, np.float32).T)
    return {"w": _tiles_lhsT(wT, D, D),
            "bias": chan_pd(bias),
            "scale": np.ascontiguousarray(np.asarray(weight_scale, np.float32))}



def ph_ffn0(k, nc, con, W, x_hbm, ncols, out_hbm, tag, hg_hbm=None):
    with ExitStack() as c2:
        S = lambda n, s, d: c2.enter_context(nc.sbuf_tensor(n, s, d))
        xn = S(f"f0_xn{tag}", [128, DT, ncols], BF16)
        ln_chan(k, nc, con, x_hbm, ncols, xn, tag + "L")
        ntt = cdiv(ncols, 128)
        cks = chunks_of(ncols, 512)
        with ExitStack() as c3:
            S3 = lambda n, s, d: c3.enter_context(nc.sbuf_tensor(n, s, d))
            PS3 = lambda n, s, d: c3.enter_context(nc.psum_tensor(n, s, d))
            wr = S3(f"f0_wr{tag}", [128, DT, 64], BF16)
            hgc = [S3(f"f0_hg{tag}{i}", [128, 512], BF16) for i in range(2)]
            hgo = [S3(f"f0_ho{tag}{i}", [128, 8, 512], BF16) for i in range(2)]
            gt = S3(f"f0_gt{tag}", [64, ncols], BF16)
            asl = [S3(f"f0_a{tag}{i}", [128, DT, 128], BF16) for i in range(2)]
            bsl = [S3(f"f0_b{tag}{i}", [128, 8, 128], BF16) for i in range(2)]
            sc = S3(f"f0_sc{tag}", [128, 64], F32)
            rep = S3(f"f0_rp{tag}", [128, 64], F32)
            m8 = S3(f"f0_m8{tag}", [128, 8], F32)
            msk = S3(f"f0_mk{tag}", [128, 64], F32)
            ge = S3(f"f0_ge{tag}", [128, 64], F32)
            gsum = S3(f"f0_gs{tag}", [128, 1], F32)
            nmax = S3(f"f0_nm{tag}", [128, 1], F32)
            gb16 = S3(f"f0_g16{tag}", [128, 64], BF16)
            gbc = [S3(f"f0_gb{tag}{i}", [128, 512], BF16) for i in range(2)]
            xold = [S3(f"f0_xo{tag}{i}", [128, 512], F32) for i in range(2)]
            xnew = [S3(f"f0_xw{tag}{i}", [128, 512], F32) for i in range(2)]
            pmm = [PS3(f"f0_p{tag}{i}", [128, 512], F32) for i in range(6)]
            paux = PS3(f"f0_px{tag}", [128, 512], F32)
            pgt = PS3(f"f0_pg{tag}", [64, 128], BF16)
            k.dma("g", wr[:], W["wr"].rearrange("kt kk m -> kk kt m"))
            for tt in range(ntt):
                c0 = tt * 128
                w = min(128, ncols - c0)
                if w < 128:
                    k.op("v", "memset", gb16[:], 0.0, writes=[gb16[:]])
                g1 = k.mm_group()
                for dt_ in range(DT):
                    g1.mm(paux[:w, :64], xn[:, dt_, c0:c0 + w], wr[:, dt_, :],
                          last=(dt_ == DT - 1),
                          reads=[xn[:], wr[:]], writes=[paux[:]])
                k.op("v", "tensor_copy", sc[:w, :], paux[:w, :64],
                     reads=[paux[:]], writes=[sc[:]])
                k.op("v", "max", out=m8[:w, :], in_=sc[:w, :],
                     reads=[sc[:]], writes=[m8[:]])
                k.op("v", "tensor_scalar_mul", nmax[:w, :], m8[:w, 0:1], -1.0,
                     reads=[m8[:]], writes=[nmax[:]])
                k.op("v", "match_replace", out=rep[:w, :], in_to_replace=m8[:w, :],
                     in_values=sc[:w, :], imm_value=-1e30,
                     reads=[m8[:], sc[:]], writes=[rep[:]])
                k.op("v", "max", out=m8[:w, :], in_=rep[:w, :],
                     reads=[rep[:]], writes=[m8[:]])
                k.op("v", "match_replace", out=rep[:w, :], in_to_replace=m8[:w, :],
                     in_values=rep[:w, :], imm_value=-1e30,
                     reads=[m8[:], rep[:]], writes=[rep[:]])
                k.op("v", "tensor_tensor", msk[:w, :], sc[:w, :], rep[:w, :], ALU.is_gt,
                     reads=[sc[:], rep[:]], writes=[msk[:]])
                k.op("a", "activation", ge[:w, :], sc[:w, :], AF.Exp, bias=nmax[:w, :],
                     reads=[sc[:], nmax[:]], writes=[ge[:]])
                k.op("v", "tensor_mul", ge[:w, :], ge[:w, :], msk[:w, :],
                     reads=[ge[:], msk[:]], writes=[ge[:]])
                k.op("v", "tensor_reduce", gsum[:w, :], ge[:w, :], AX.X, ALU.add,
                     reads=[ge[:]], writes=[gsum[:]])
                k.op("v", "reciprocal", gsum[:w, :], gsum[:w, :],
                     reads=[gsum[:]], writes=[gsum[:]])
                k.op("v", "tensor_scalar_mul", gb16[:w, :], ge[:w, :], gsum[:w, :],
                     reads=[ge[:], gsum[:]], writes=[gb16[:]])
                k.op("t", "transpose", pgt[:], gb16[:, :], con.id128b[:],
                     reads=[gb16[:], con.id128b[:]], writes=[pgt[:]])
                k.op("a", "copy", gt[:, c0:c0 + w], pgt[:, :w],
                     reads=[pgt[:]], writes=[gt[:]])
            rot = [0]
            for krt in range(8):
                buf = krt % 2
                k.dma("g", asl[buf][:], W["a"][krt].rearrange("kt kk m -> kk kt m"))
                for ci, (c0, w) in enumerate(cks):
                    bank = rot[0] % 6
                    rot[0] += 1
                    gm = k.mm_group()
                    for kt in range(DT):
                        gm.mm(pmm[bank][:, :w], asl[buf][:, kt, :],
                              xn[:, kt, c0:c0 + w], last=(kt == DT - 1),
                              reads=[asl[buf][:], xn[:]], writes=[pmm[bank][:]])
                    gx = k.mm_group()
                    gx.mm(paux[:, :w], con.moexp[:, krt, :], gt[:, c0:c0 + w],
                          last=True, reads=[con.moexp[:], gt[:]], writes=[paux[:]])
                    cb = ci % 2
                    k.op("a", "copy", gbc[cb][:, :w], paux[:, :w],
                         reads=[paux[:]], writes=[gbc[cb][:]])
                    k.op("v", "tensor_tensor", hgc[cb][:, :w], pmm[bank][:, :w],
                         gbc[cb][:, :w], ALU.mult,
                         reads=[pmm[bank][:], gbc[cb][:]], writes=[hgc[cb][:]])
                    k.dma("a", hg_hbm[krt, :, c0:c0 + w], hgc[cb][:, :w])
            xhr = x_hbm.rearrange("dt p t -> p dt t")
            ohr = out_hbm.rearrange("dt p t -> p dt t")
            for dt_ in range(DT):
                buf = dt_ % 2
                k.dma("g", bsl[buf][:], W["b"][dt_].rearrange("kt kk m -> kk kt m"))
                for ci, (c0, w) in enumerate(cks):
                    bank = rot[0] % 6
                    rot[0] += 1
                    cb2 = (dt_ * len(cks) + ci) % 2
                    k.dma("s", hgo[cb2][:, :, :w], hg_hbm[:, :, c0:c0 + w].rearrange("kr p t -> p kr t"))
                    gm = k.mm_group()
                    for krt in range(8):
                        gm.mm(pmm[bank][:, :w], bsl[buf][:, krt, :],
                              hgo[cb2][:, krt, :w], last=(krt == 7),
                              reads=[bsl[buf][:], hgo[cb2][:]], writes=[pmm[bank][:]])
                    cb = ci % 2
                    k.dma("s", xold[cb][:, :w], xhr[:, dt_, c0:c0 + w])
                    k.op("v", "tensor_add", xnew[cb][:, :w], pmm[bank][:, :w],
                         xold[cb][:, :w],
                         reads=[pmm[bank][:], xold[cb][:]], writes=[xnew[cb][:]])
                    k.dma("a", ohr[:, dt_, c0:c0 + w], xnew[cb][:, :w])
            k.phase_barrier()


def host_prep_ffn0(A, B, Wr):
    A = np.asarray(A, np.float32)
    B = np.asarray(B, np.float32)
    Wr = np.asarray(Wr, np.float32)
    A_t = A.transpose(2, 0, 1).reshape(D, 64 * 16)
    B_t = B.transpose(0, 2, 1).reshape(64 * 16, D)
    wrT = np.ascontiguousarray(Wr.T).reshape(DT, 128, 64)
    return {"a": _tiles_lhsT(A_t, D, 1024),
            "b": _tiles_lhsT(B_t, 1024, D),
            "wr": np.ascontiguousarray(wrT)}



def softplus_ops(k, out, z, t1, rz, rt, ro):
    k.op("a", "activation", t1, z, AF.Abs, reads=[rz], writes=[rt])
    k.op("a", "activation", t1, t1, AF.Exp, scale=-1.0, reads=[rt], writes=[rt])
    k.op("a", "activation", t1, t1, AF.Ln, bias=1.0, reads=[rt], writes=[rt])
    k.op("v", "tensor_scalar", out, z, 0.0, None, ALU.max, reads=[rz], writes=[ro])
    k.op("v", "tensor_add", out, out, t1, reads=[ro, rt], writes=[ro])


def ph_mamba1(k, nc, con, W, x1_hbm, Q, ST, tag="m1"):
    QP = pad128(Q)
    TA = Q + 3
    NCH = (Q - 1) // 128
    with ExitStack() as c2:
        S = lambda n, s, d: c2.enter_context(nc.sbuf_tensor(n, s, d))
        cks = chunks_of(TA, 512)
        dtpb = S(f"mb_dpb{tag}", [64, QP], BF16)
        at = S(f"mb_at{tag}", [64, QP], F32)
        ctot = S(f"mb_ct{tag}", [64, NCH], F32)
        lamc = S(f"mb_lc{tag}", [64, NCH], F32)
        dpe = S(f"mb_dp{tag}", [64, NCH + 1], F32)
        ones16 = S(f"mb_on{tag}", [64, NCH], F32)
        ones128 = S(f"mb_o1{tag}", [64, 128], F32)
        acol = S(f"mb_ac{tag}", [64, 1], F32)
        with ExitStack() as c3:
            S3 = lambda n, s, d: c3.enter_context(nc.sbuf_tensor(n, s, d))
            PS3 = lambda n, s, d: c3.enter_context(nc.psum_tensor(n, s, d))
            xn = S3(f"mb_xn{tag}", [128, DT, TA], BF16)
            ln_chan(k, nc, con, x1_hbm, TA, xn, tag + "L")
            wsl = [S3(f"mb_w{tag}0", [128, DT, 128], BF16)] * 2
            tb = [S3(f"mb_tb{tag}0", [128, TA], F32)] * 2
            cvb = [S3(f"mb_cv{tag}0", [128, QP], F32)] * 2
            ob16 = [S3(f"mb_o16{tag}0", [128, QP], BF16)] * 2
            dxb = [S3(f"mb_dx{tag}0", [128, QP], BF16)] * 2
            stg = [S3(f"mb_st{tag}0", [128, QP], BF16)] * 2
            convw = S3(f"mb_cw{tag}", [128, 48, 4], F32)
            convb = S3(f"mb_cb{tag}", [128, 48], F32)
            dtb = S3(f"mb_db{tag}", [64, 1], F32)
            alog = S3(f"mb_al{tag}", [64, 1], F32)
            mq0 = S3(f"mb_mq{tag}", [64, 1], F32)

            pmm = [PS3(f"mb_p{tag}{i}", [128, 512], F32) for i in range(6)]
            pdt = PS3(f"mb_pd{tag}", [128, 512], F32)
            k.dma("s", convw[:], W["convw"])
            k.dma("s", convb[:], W["convb"])
            k.dma("s", dtb[:], W["dtb"])
            k.dma("s", alog[:], W["alog"])
            k.dma("s", mq0[:], W["mq0"])
            rot = [0]

            def inproj_tile(ot, into, into_off):
                buf = ot % 2
                k.dma("g", wsl[buf][:], W["wproj"][ot].rearrange("kt kk m -> kk kt m"))
                for ci, (c0, w) in enumerate(cks):
                    bank = rot[0] % 6
                    rot[0] += 1
                    gm = k.mm_group()
                    for kt in range(DT):
                        gm.mm(pmm[bank][:, :w], wsl[buf][:, kt, :],
                              xn[:, kt, c0:c0 + w], last=(kt == DT - 1),
                              reads=[wsl[buf][:], xn[:]], writes=[pmm[bank][:]])
                    k.op("a", "copy", into[:, into_off + c0:into_off + c0 + w],
                         pmm[bank][:, :w], reads=[pmm[bank][:]], writes=[into[:]])

            def conv_silu(ot, buf):
                ct = ot - 32
                wc = convw[:, ct, :]
                k.op("v", "tensor_scalar", cvb[buf][:, :Q], tb[buf][:, 0:Q],
                     wc[:, 0:1], convb[:, ct:ct + 1], ALU.mult, ALU.add,
                     reads=[tb[buf][:], convw[:], convb[:]], writes=[cvb[buf][:]])
                for kk in range(1, 4):
                    k.op("v", "scalar_tensor_tensor", cvb[buf][:, :Q],
                         tb[buf][:, kk:kk + Q], wc[:, kk:kk + 1], cvb[buf][:, :Q],
                         ALU.mult, ALU.add,
                         reads=[tb[buf][:], convw[:], cvb[buf][:]],
                         writes=[cvb[buf][:]])
                if QP > Q:
                    k.op("v", "memset", cvb[buf][:, Q:QP], 0.0, writes=[cvb[buf][:]])
                k.op("a", "activation", cvb[buf][:, :], cvb[buf][:, :], AF.Silu,
                     reads=[cvb[buf][:]], writes=[cvb[buf][:]])

            dtpf = S3(f"mb_dtf{tag}", [64, QP], F32)
            inproj_tile(80, tb[0], 0)
            sp1 = cvb[0][0:64, 0:QP]
            k.op("v", "tensor_scalar_add", dtpf[:, :Q], tb[0][0:64, 3:3 + Q],
                 dtb[:], reads=[tb[0][:], dtb[:]], writes=[dtpf[:]])
            softplus_ops(k, dtpf[:, :Q], dtpf[:, :Q], sp1[:, :Q],
                         dtpf[:], cvb[0][:], dtpf[:])
            if QP > Q:
                k.op("v", "memset", dtpf[:, Q:QP], 0.0, writes=[dtpf[:]])
            k.op("v", "tensor_mul", dtpf[:, 0:1], dtpf[:, 0:1], mq0[:],
                 reads=[dtpf[:], mq0[:]], writes=[dtpf[:]])
            k.op("a", "activation", acol[:], alog[:], AF.Exp,
                 reads=[alog[:]], writes=[acol[:]])
            k.op("v", "tensor_scalar_mul", acol[:], acol[:], -1.0,
                 reads=[acol[:]], writes=[acol[:]])
            k.op("v", "tensor_scalar_mul", at[:, :], dtpf[:, :], acol[:],
                 reads=[dtpf[:], acol[:]], writes=[at[:]])
            k.op("v", "tensor_copy", dtpb[:], dtpf[:], reads=[dtpf[:]], writes=[dtpb[:]])
            k.dma("a", ST["at"], at[:])
            k.op("v", "memset", ones16[:], 1.0, writes=[ones16[:]])
            k.op("v", "memset", ones128[:], 1.0, writes=[ones128[:]])
            k.op("v", "tensor_reduce", ctot[:].unsqueeze(2),
                 at[:, 0:NCH * 128].rearrange("p (c w) -> p c w", w=128), AX.X, ALU.add,
                 reads=[at[:]], writes=[ctot[:]])
            k.op("a", "activation", lamc[:], ctot[:], AF.Exp,
                 reads=[ctot[:]], writes=[lamc[:]])
            k.op("v", "memset", dpe[:, 0:1], 0.0, writes=[dpe[:]])
            k.op("v", "tensor_tensor_scan", dpe[:, 1:NCH + 1], ones16[:], ctot[:],
                 0.0, ALU.mult, ALU.add, reads=[ones16[:], ctot[:]], writes=[dpe[:]])
            k.op("a", "activation", dpe[:], dpe[:], AF.Exp,
                 reads=[dpe[:]], writes=[dpe[:]])
            k.dma("a", ST["dpe"], dpe[:])
            k.op("v", "tensor_tensor", sp1[:, 0:1], dpe[:, NCH:NCH + 1],
                 lamc[:, NCH - 1:NCH],
                 ALU.mult, reads=[dpe[:], lamc[:]], writes=[sp1[:]])
            if "sum" in ST:
                lam8 = S3(f"mb_l8{tag}", [64, 8], F32)
                k.op("v", "tensor_copy", lam8[:],
                     sp1[:, 0:1].broadcast_to((64, 8)),
                     reads=[cvb[0][:]], writes=[lam8[:]])
                k.dma("a", ST["sum"][0:64, 4096:4104], lam8[:])
            else:
                k.dma("a", ST["lamtot"], sp1[:, 0:1])

            for ot in range(64, 80):
                buf = ot % 2
                inproj_tile(ot, tb[buf], 0)
                conv_silu(ot, buf)
                k.op("v", "tensor_copy", ob16[buf][:], cvb[buf][:],
                     reads=[cvb[buf][:]], writes=[ob16[buf][:]])
                g = ot - 64
                if g < 8:
                    k.dma("a", ST["bb"][g], ob16[buf][:])
                    for qb in range(QP // 128):
                        k.dma("s", stg[buf][:, qb * 128:(qb + 1) * 128],
                              ob16[buf][:, qb * 128:(qb + 1) * 128], transpose=True)
                    k.dma("a", ST["bbT"][:, :, g, :].rearrange("qb q pin -> q qb pin"),
                          stg[buf][:].rearrange("q (qb pin) -> q qb pin", pin=128))
                else:
                    k.dma("a", ST["cc"][g - 8], ob16[buf][:])

            for ot in range(32, 64):
                buf = ot % 2
                j = ot - 32
                inproj_tile(ot, tb[buf], 0)
                conv_silu(ot, buf)
                k.op("v", "tensor_copy", ob16[buf][:], cvb[buf][:],
                     reads=[cvb[buf][:]], writes=[ob16[buf][:]])
                k.dma("a", ST["xs"][j], ob16[buf][:])
                for (c0, wdt) in chunks_of(QP, 512):
                    gx = k.mm_group()
                    gx.mm(pdt[:, :wdt], con.dtexp[:, j, :], dtpb[:, c0:c0 + wdt],
                          last=True, reads=[con.dtexp[:], dtpb[:]], writes=[pdt[:]])
                    k.op("v", "tensor_tensor", dxb[buf][:, c0:c0 + wdt],
                         cvb[buf][:, c0:c0 + wdt], pdt[:, :wdt], ALU.mult,
                         reads=[cvb[buf][:], pdt[:]], writes=[dxb[buf][:]])
                for qb in range(QP // 128):
                    k.dma("s", stg[buf][:, qb * 128:(qb + 1) * 128],
                          dxb[buf][:, qb * 128:(qb + 1) * 128], transpose=True)
                k.dma("a", ST["dtxT"][:, :, j, :].rearrange("qb q pin -> q qb pin"),
                      stg[buf][:].rearrange("q (qb pin) -> q qb pin", pin=128))

            for ot in range(0, 32):
                buf = ot % 2
                inproj_tile(ot, tb[buf], 0)
                if QP > Q:
                    k.op("v", "memset", ob16[buf][:, Q:QP], 0.0,
                         writes=[ob16[buf][:]])
                k.op("v", "tensor_copy", ob16[buf][:, 0:Q], tb[buf][:, 3:3 + Q],
                     reads=[tb[buf][:]], writes=[ob16[buf][:]])
                k.dma("a", ST["gate"][ot], ob16[buf][:])
            k.phase_barrier()
        with ExitStack() as c4:
            S4 = lambda n, s, d: c4.enter_context(nc.sbuf_tensor(n, s, d))
            PS4 = lambda n, s, d: c4.enter_context(nc.psum_tensor(n, s, d))
            btl = [S4(f"mb_btl{tag}{i}", [128, 8, 128], BF16) for i in range(2)]
            dxl = [S4(f"mb_dxl{tag}{i}", [128, DT, 128], BF16) for i in range(2)]
            btw = [S4(f"mb_btw{tag}{i}", [128, 128], BF16) for i in range(2)]
            cum = S4(f"mb_cum{tag}", [64, 128], F32)
            wex = S4(f"mb_wx{tag}", [64, 128], F32)
            wexT = S4(f"mb_wxt{tag}", [128, 64], F32)
            ubuf = S4(f"mb_ub{tag}", [128, 4096], F32)
            ssum = S4(f"mb_ss{tag}", [128, 4096], F32)
            s16 = S4(f"mb_s16{tag}", [128, 4096], BF16)
            lrow = S4(f"mb_lr{tag}", [128, 64], F32)
            lamT = S4(f"mb_lT{tag}", [NCH, 64], F32)
            pu = [PS4(f"mb_pu{tag}0", [128, 2048], F32)] * 2
            pw = PS4(f"mb_pw{tag}", [128, 64], F32)
            plr = PS4(f"mb_plr{tag}", [128, 64], F32)
            k.op("v", "memset", ssum[:], 0.0, writes=[ssum[:]])
            k.op("t", "transpose", pw[:NCH, :], lamc[:, :], con.id64f[:],
                 reads=[lamc[:], con.id64f[:]], writes=[pw[:]])
            k.op("a", "copy", lamT[:], pw[:NCH, :64], reads=[pw[:]], writes=[lamT[:]])
            for c in range(NCH):
                buf = c % 2
                k.op("v", "tensor_copy", s16[:], ssum[:],
                     reads=[ssum[:]], writes=[s16[:]])
                k.dma("a", ST["st"][c], s16[:])
                k.dma("s", btl[buf][:], ST["bbT"][c])
                k.dma("s", dxl[buf][:], ST["dtxT"][c])
                k.op("v", "tensor_tensor_scan", cum[:], ones128[:],
                     at[:, c * 128:(c + 1) * 128], 0.0, ALU.mult, ALU.add,
                     reads=[ones128[:], at[:]], writes=[cum[:]])
                k.op("a", "activation", wex[:], cum[:], AF.Exp, scale=-1.0,
                     bias=ctot[:, c:c + 1], reads=[cum[:], ctot[:]], writes=[wex[:]])
                k.op("t", "transpose", pw[:, :], wex[:, :], con.id64f[:],
                     reads=[wex[:], con.id64f[:]], writes=[pw[:]])
                k.op("a", "copy", wexT[:, :], pw[:, :],
                     reads=[pw[:]], writes=[wexT[:]])
                for half in range(2):
                    for hh in range(32):
                        h = half * 32 + hh
                        g = h // 8
                        k.op("v", "tensor_scalar_mul", btw[h % 2][:],
                             btl[buf][:, g, :], wexT[:, h:h + 1],
                             reads=[btl[buf][:], wexT[:]], writes=[btw[h % 2][:]])
                        gmm = k.mm_group()
                        gmm.mm(pu[half][:, hh * 64:(hh + 1) * 64],
                               btw[h % 2][:], dxl[buf][:].rearrange(
                                   "q dt pin -> q (dt pin)")[:, h * 64:(h + 1) * 64],
                               last=True,
                               reads=[btw[h % 2][:], dxl[buf][:]],
                               writes=[pu[half][:]])
                    k.op("v", "tensor_copy", ubuf[:, half * 2048:(half + 1) * 2048],
                         pu[half][:], reads=[pu[half][:]],
                         writes=[ubuf[:]])
                glr = k.mm_group()
                glr.mm(plr[:, :], con.id128f[:NCH, c:c + 1].broadcast_to((NCH, 128)),
                       lamT[:, :], last=True, reads=[con.id128f[:], lamT[:]],
                       writes=[plr[:]])
                k.op("a", "copy", lrow[:], plr[:], reads=[plr[:]], writes=[lrow[:]])
                k.op("v", "tensor_tensor",
                     ssum[:].rearrange("p (h e) -> p h e", e=64),
                     ssum[:].rearrange("p (h e) -> p h e", e=64),
                     lrow[:].rearrange("p (h o) -> p h o", o=1).broadcast_to((128, 64, 64)),
                     ALU.mult, reads=[ssum[:], lrow[:]], writes=[ssum[:]])
                k.op("v", "tensor_add", ssum[:], ssum[:], ubuf[:],
                     reads=[ssum[:], ubuf[:]], writes=[ssum[:]])
            k.op("v", "tensor_copy", s16[:], ssum[:], reads=[ssum[:]], writes=[s16[:]])
            k.dma("a", ST["st"][NCH], s16[:])
            if "sum" in ST:
                k.dma("a", ST["sum"][:, 0:4096], ssum[:])
            else:
                k.dma("a", ST["utot"], ssum[:])
            k.phase_barrier()
        k.phase_barrier()


def ph_mamba2(k, nc, con, W, ST, SG, x1_hbm, Q, x2_hbm, tag="m2"):
    QP = pad128(Q)
    NCH = (Q - 1) // 128
    B2CH = QP // 128
    with ExitStack() as c2:
        S = lambda n, s, d: c2.enter_context(nc.sbuf_tensor(n, s, d))
        PS = lambda n, s, d: c2.enter_context(nc.psum_tensor(n, s, d))
        at = S(f"m2_at{tag}", [64, QP], F32)
        sin = S(f"m2_si{tag}", [128, 4096], F32)
        dpeT = S(f"m2_dT{tag}", [B2CH, 64], F32)
        nw = S(f"m2_nw{tag}", [128, DT], F32)
        drep = S(f"m2_dr{tag}", [128, DT], F32)
        ones128 = S(f"m2_o1{tag}", [64, 128], F32)
        k.dma("s", at[:], ST["at"])
        k.dma("s", nw[:], W["nw"])
        k.dma("s", drep[:], W["drep"])
        k.op("v", "memset", ones128[:], 1.0, writes=[ones128[:]])
        with ExitStack() as c3:
            S3 = lambda n, s, d: c3.enter_context(nc.sbuf_tensor(n, s, d))
            PS3 = lambda n, s, d: c3.enter_context(nc.psum_tensor(n, s, d))
            lamg = S3(f"m2_lg{tag}", [64, 8], F32)
            fmc = S3(f"m2_fm{tag}", [64, 8], F32)
            fmc128 = S3(f"m2_f1{tag}", [128, 8], F32)
            c2t = S3(f"m2_c2{tag}", [8, 64], F32)
            cc2 = S3(f"m2_cc{tag}", [64, 8], F32)
            lrow = S3(f"m2_lr{tag}", [128, 64], F32)
            uj = S3(f"m2_uj{tag}", [128, 4096], F32)
            dpe2 = S3(f"m2_dp{tag}", [64, B2CH], F32)
            ptr = PS3(f"m2_pt{tag}", [128, 128], F32)
            if "lam3d" in SG:
                lam3 = S3(f"m2_l3{tag}", [64, 8, 8], F32)
                k.dma("s", lam3[:], SG["lam3d"])
                k.op("v", "tensor_copy", lamg[:], lam3[:, :, 0],
                     reads=[lam3[:]], writes=[lamg[:]])
            else:
                k.dma("s", lamg[:], SG["lam"])
            k.dma("s", fmc[:], SG["fmc"])
            k.dma("s", fmc128[:], SG["fmc128"])
            k.dma("s", dpe2[:], ST["dpe"])
            k.op("v", "memset", sin[:], 0.0, writes=[sin[:]])
            k.op("v", "tensor_tensor", cc2[:], lamg[:], fmc[:], ALU.mult,
                 reads=[lamg[:], fmc[:]], writes=[cc2[:]])
            k.op("v", "tensor_scalar", lamg[:], fmc[:], -1.0, 1.0, ALU.mult, ALU.add,
                 reads=[fmc[:]], writes=[lamg[:]])
            k.op("v", "tensor_add", cc2[:], cc2[:], lamg[:],
                 reads=[cc2[:], lamg[:]], writes=[cc2[:]])
            k.op("t", "transpose", ptr[:8, :64], cc2[:], con.id64f[:],
                 reads=[cc2[:], con.id64f[:]], writes=[ptr[:]])
            k.op("a", "copy", c2t[:], ptr[:8, :64], reads=[ptr[:]], writes=[c2t[:]])
            for j in range(8):
                gm = k.mm_group()
                gm.mm(ptr[:, :64], con.id128f[:8, j:j + 1].broadcast_to((8, 128)),
                      c2t[:], last=True, reads=[con.id128f[:], c2t[:]], writes=[ptr[:]])
                k.op("a", "copy", lrow[:], ptr[:, :64], reads=[ptr[:]], writes=[lrow[:]])
                k.op("v", "tensor_tensor",
                     sin[:].rearrange("p (h e) -> p h e", e=64),
                     sin[:].rearrange("p (h e) -> p h e", e=64),
                     lrow[:].rearrange("p (h o) -> p h o", o=1).broadcast_to((128, 64, 64)),
                     ALU.mult, reads=[sin[:], lrow[:]], writes=[sin[:]])
                k.dma("s", uj[:], SG["ut"][j])
                k.op("v", "scalar_tensor_tensor", sin[:], uj[:], fmc128[:, j:j + 1],
                     sin[:], ALU.mult, ALU.add,
                     reads=[uj[:], fmc128[:], sin[:]], writes=[sin[:]])
            k.op("t", "transpose", ptr[:B2CH, :64], dpe2[:], con.id64f[:],
                 reads=[dpe2[:], con.id64f[:]], writes=[ptr[:]])
            k.op("a", "copy", dpeT[:], ptr[:B2CH, :64], reads=[ptr[:]], writes=[dpeT[:]])
            k.phase_barrier()
        with ExitStack() as c4:
            S4 = lambda n, s, d: c4.enter_context(nc.sbuf_tensor(n, s, d))
            PS4 = lambda n, s, d: c4.enter_context(nc.psum_tensor(n, s, d))
            dtxl = [S4(f"m2_dx{tag}{i}", [128, DT, 128], BF16) for i in range(2)]
            xsl = [S4(f"m2_xs{tag}{i}", [128, DT, 128], BF16) for i in range(2)]
            gal = [S4(f"m2_ga{tag}{i}", [128, DT, 128], BF16) for i in range(2)]
            bbl = [S4(f"m2_bb{tag}{i}", [128, 8, 128], BF16) for i in range(2)]
            ccl = [S4(f"m2_cl{tag}{i}", [128, 8, 128], BF16) for i in range(2)]
            stl = [S4(f"m2_sl{tag}{i}", [128, 4096], BF16) for i in range(2)]
            scf = S4(f"m2_sc{tag}", [128, 4096], F32)
            scb = S4(f"m2_sb{tag}", [128, 4096], BF16)
            tmp4k = S4(f"m2_t4{tag}", [128, 4096], F32)
            drow = S4(f"m2_dw{tag}", [128, 64], F32)
            cum = S4(f"m2_cu{tag}", [64, 128], F32)
            ncT = S4(f"m2_nc{tag}", [128, 64], F32)
            expc = S4(f"m2_ex{tag}", [64, 128], F32)
            gsb = [S4(f"m2_gs{tag}{i}", [128, 128], BF16) for i in range(8)]
            ahb4 = S4(f"m2_ah{tag}", [128, 512], BF16)
            esb4 = S4(f"m2_es{tag}", [64, 512], F32)
            t2b4 = S4(f"m2_t2{tag}", [64, 512], F32)
            y44 = S4(f"m2_y4{tag}", [128, 512], F32)
            ybuf = S4(f"m2_yb{tag}", [128, DT, 128], F32)
            ysq = S4(f"m2_yq{tag}", [128, DT, 128], BF16)
            gsil = [S4(f"m2_gl{tag}{i}", [128, 128], BF16) for i in range(2)]
            t3b = [S4(f"m2_t3{tag}{i}", [128, 128], F32) for i in range(2)]
            ynb = S4(f"m2_yn{tag}", [128, DT, 128], BF16)
            rsr = S4(f"m2_rs{tag}", [1, 1024], F32)
            pM = PS4(f"m2_pM{tag}", [128, 512], F32)
            pG = PS4(f"m2_pG{tag}", [128, 128], F32)
            pY = PS4(f"m2_pY{tag}", [64, 512], F32)
            pZ = PS4(f"m2_pZ{tag}", [64, 512], F32)
            pE = PS4(f"m2_pE{tag}", [64, 512], F32)
            pDP = PS4(f"m2_pD{tag}", [128, 64], F32)
            pRs = [PS4(f"m2_pR{tag}{i}", [1, 512], F32) for i in range(2)]
            bbr = ST["bb"].rearrange("g p t -> p g t")
            ccr = ST["cc"].rearrange("g p t -> p g t")
            xsr = ST["xs"].rearrange("j p t -> p j t")
            gar = ST["gate"].rearrange("j p t -> p j t")
            ynr = ST["yn"].rearrange("j p t -> p j t")
            for c in range(B2CH):
                buf = c % 2
                c0 = c * 128
                k.dma("s", dtxl[buf][:], ST["dtxT"][c])
                k.dma("s", xsl[buf][:], xsr[:, :, c0:c0 + 128])
                k.dma("s", gal[buf][:], gar[:, :, c0:c0 + 128])
                k.dma("s", bbl[buf][:], bbr[:, :, c0:c0 + 128])
                k.dma("s", ccl[buf][:], ccr[:, :, c0:c0 + 128])
                k.dma("s", stl[buf][:], ST["st"][min(c, NCH)])
                gm = k.mm_group()
                gm.mm(pDP[:, :], con.id128f[:B2CH, c:c + 1].broadcast_to((B2CH, 128)),
                      dpeT[:], last=True, reads=[con.id128f[:], dpeT[:]], writes=[pDP[:]])
                k.op("a", "copy", drow[:], pDP[:], reads=[pDP[:]], writes=[drow[:]])
                k.op("v", "tensor_tensor",
                     tmp4k[:].rearrange("p (h e) -> p h e", e=64),
                     sin[:].rearrange("p (h e) -> p h e", e=64),
                     drow[:].rearrange("p (h o) -> p h o", o=1).broadcast_to((128, 64, 64)),
                     ALU.mult, reads=[sin[:], drow[:]], writes=[tmp4k[:]])
                k.op("v", "tensor_add", scf[:], tmp4k[:], stl[buf][:],
                     reads=[tmp4k[:], stl[buf][:]], writes=[scf[:]])
                k.op("v", "tensor_copy", scb[:], scf[:], reads=[scf[:]], writes=[scb[:]])
                k.op("v", "tensor_tensor_scan", cum[:], ones128[:],
                     at[:, c0:c0 + 128], 0.0, ALU.mult, ALU.add,
                     reads=[ones128[:], at[:]], writes=[cum[:]])
                k.op("t", "transpose", pM[:, :64], cum[:], con.id64f[:],
                     reads=[cum[:], con.id64f[:]], writes=[pM[:]])
                k.op("a", "activation", ncT[:], pM[:, :64], AF.Copy, scale=-1.0,
                     reads=[pM[:]], writes=[ncT[:]])
                k.op("a", "activation", expc[:], cum[:], AF.Exp,
                     reads=[cum[:]], writes=[expc[:]])
                for g in range(8):
                    gg = k.mm_group()
                    gg.mm(pG[:, :], bbl[buf][:, g, :], ccl[buf][:, g, :], last=True,
                          reads=[bbl[buf][:], ccl[buf][:]], writes=[pG[:]])
                    k.op("v", "tensor_copy", gsb[g][:], pG[:],
                         reads=[pG[:]], writes=[gsb[g][:]])
                dtxf = dtxl[buf][:].rearrange("q dt pin -> q (dt pin)")
                for hq in range(HH // 4):
                    h0 = hq * 4
                    g = h0 // 8
                    gm1 = k.mm_group()
                    for i in range(4):
                        h = h0 + i
                        gm1.mm(pM[:, i * 128:(i + 1) * 128],
                               con.id64f[:, h:h + 1].broadcast_to((64, 128)),
                               cum[:], last=(i == 3), single=True,
                               reads=[con.id64f[:], cum[:]], writes=[pM[:]])
                    for i in range(4):
                        h = h0 + i
                        k.op("a", "activation", ahb4[:, i * 128:(i + 1) * 128],
                             pM[:, i * 128:(i + 1) * 128], AF.Exp,
                             bias=ncT[:, h:h + 1],
                             reads=[pM[:], ncT[:]], writes=[ahb4[:]])
                    k.op("v", "tensor_tensor",
                         ahb4[:].rearrange("p (r t) -> p r t", t=128),
                         ahb4[:].rearrange("p (r t) -> p r t", t=128),
                         gsb[g][:].unsqueeze(1).broadcast_to((128, 4, 128)),
                         ALU.mult, reads=[ahb4[:], gsb[g][:]], writes=[ahb4[:]])
                    k.op("g", "affine_select", ahb4[:], ahb4[:],
                         pattern=[[0, 4], [1, 128]], compare_op=ALU.is_ge, fill=0.0,
                         base=0, channel_multiplier=-1,
                         reads=[ahb4[:]], writes=[ahb4[:]])
                    gy = k.mm_group()
                    for i in range(4):
                        h = h0 + i
                        gy.mm(pY[:, i * 128:(i + 1) * 128],
                              dtxf[:, h * 64:(h + 1) * 64],
                              ahb4[:, i * 128:(i + 1) * 128], last=(i == 3),
                              single=True,
                              reads=[dtxl[buf][:], ahb4[:]], writes=[pY[:]])
                    gz = k.mm_group()
                    for i in range(4):
                        h = h0 + i
                        gz.mm(pZ[:, i * 128:(i + 1) * 128],
                              scb[:, h * 64:(h + 1) * 64], ccl[buf][:, h // 8, :],
                              last=(i == 3), single=True,
                              reads=[scb[:], ccl[buf][:]], writes=[pZ[:]])
                    ge = k.mm_group()
                    for i in range(4):
                        h = h0 + i
                        ge.mm(pE[:, i * 128:(i + 1) * 128],
                              con.id64f[:, h:h + 1].broadcast_to((64, 64)),
                              expc[:], last=(i == 3), single=True,
                              reads=[con.id64f[:], expc[:]], writes=[pE[:]])
                    k.op("a", "copy", esb4[:], pE[:, :],
                         reads=[pE[:]], writes=[esb4[:]])
                    k.op("v", "tensor_tensor", t2b4[:], pZ[:, :], esb4[:], ALU.mult,
                         reads=[pZ[:], esb4[:]], writes=[t2b4[:]])
                    k.op("v", "tensor_add", y44[0:64, :], t2b4[:], pY[:, :],
                         reads=[t2b4[:], pY[:]], writes=[y44[:]])
                    k.op("a", "copy", y44[64:128, :], y44[0:64, :],
                         reads=[y44[:]], writes=[y44[:]])
                    for i in range(4):
                        h = h0 + i
                        j, half = h // 2, h % 2
                        pb_ = half * 64
                        k.op("v", "scalar_tensor_tensor",
                             ybuf[pb_:pb_ + 64, j, :],
                             xsl[buf][pb_:pb_ + 64, j, :], drep[pb_:pb_ + 64, j:j + 1],
                             y44[pb_:pb_ + 64, i * 128:(i + 1) * 128], ALU.mult, ALU.add,
                             reads=[xsl[buf][:], drep[:], y44[:]], writes=[ybuf[:]])
                k.op("a", "activation", ysq[:].rearrange("p j t -> p (j t)"),
                     ybuf[:].rearrange("p j t -> p (j t)"), AF.Square,
                     reads=[ybuf[:]], writes=[ysq[:]])
                for g in range(8):
                    gr = k.mm_group()
                    for jj in range(4):
                        gr.mm(pRs[g // 4][:, (g % 4) * 128:(g % 4 + 1) * 128],
                              con.onescol_b[:],
                              ysq[:, 4 * g + jj, :], last=(jj == 3),
                              reads=[con.onescol_b[:], ysq[:]],
                              writes=[pRs[g // 4][:]])
                for hf in range(2):
                    k.op("v", "tensor_scalar", rsr[:, hf * 512:(hf + 1) * 512],
                         pRs[hf][:, :], 1.0 / 512, EPS,
                         ALU.mult, ALU.add, reads=[pRs[hf][:]], writes=[rsr[:]])
                k.op("a", "activation", rsr[:], rsr[:], AF.Ln,
                     reads=[rsr[:]], writes=[rsr[:]])
                k.op("a", "activation", rsr[:], rsr[:], AF.Exp, scale=-0.5,
                     reads=[rsr[:]], writes=[rsr[:]])
                for j in range(DT):
                    jb = j % 2
                    g = j // 4
                    gb = k.mm_group()
                    gb.mm(pG[:, :], con.ones1f[:], rsr[0:1, g * 128:(g + 1) * 128],
                          last=True, reads=[con.ones1f[:], rsr[:]], writes=[pG[:]])
                    k.op("a", "activation", gsil[jb][:], gal[buf][:, j, :], AF.Silu,
                         reads=[gal[buf][:]], writes=[gsil[jb][:]])
                    k.op("v", "tensor_tensor", t3b[jb][:], ybuf[:, j, :], pG[:, :],
                         ALU.mult, reads=[ybuf[:], pG[:]], writes=[t3b[jb][:]])
                    k.op("v", "scalar_tensor_tensor", ynb[:, j, :], t3b[jb][:],
                         nw[:, j:j + 1], gsil[jb][:], ALU.mult, ALU.mult,
                         reads=[t3b[jb][:], nw[:], gsil[jb][:]], writes=[ynb[:]])
                k.dma("a", ynr[:, :, c0:c0 + 128], ynb[:])
            k.phase_barrier()
        with ExitStack() as c5:
            S5 = lambda n, s, d: c5.enter_context(nc.sbuf_tensor(n, s, d))
            PS5 = lambda n, s, d: c5.enter_context(nc.psum_tensor(n, s, d))
            wsl = [S5(f"m2_w{tag}{i}", [128, DT, 128], BF16) for i in range(2)]
            ynl = [S5(f"m2_yl{tag}{i}", [128, DT, 512], BF16) for i in range(2)]
            xold = [S5(f"m2_xo{tag}{i}", [128, 512], F32) for i in range(2)]
            xnew = [S5(f"m2_xw{tag}{i}", [128, 512], F32) for i in range(2)]
            zet = S5(f"m2_ze{tag}", [128, 128], F32)
            pso = [PS5(f"m2_p{tag}{i}", [128, 512], F32) for i in range(8)]
            cks = chunks_of(Q, 512)
            xhr = x1_hbm.rearrange("dt p t -> p dt t")
            ohr = x2_hbm.rearrange("dt p t -> p dt t")
            ynr = ST["yn"].rearrange("j p t -> p j t")
            for ot in range(DT):
                buf = ot % 2
                k.dma("g", wsl[buf][:], W["wout"][ot].rearrange("kt kk m -> kk kt m"))
                for ci, (c0, w) in enumerate(cks):
                    cb = ci % 2
                    k.dma("s", ynl[cb][:, :, :w], ynr[:, :, c0:c0 + w])
                    bank = ((ot % 2) * 4 + ci) % 8
                    gm = k.mm_group()
                    for kt in range(DT):
                        gm.mm(pso[bank][:, :w], wsl[buf][:, kt, :],
                              ynl[cb][:, kt, :w], last=(kt == DT - 1),
                              reads=[wsl[buf][:], ynl[cb][:]], writes=[pso[bank][:]])
                    k.dma("s", xold[cb][:, :w], xhr[:, ot, 3 + c0:3 + c0 + w])
                    k.op("v", "tensor_add", xnew[cb][:, :w], pso[bank][:, :w],
                         xold[cb][:, :w],
                         reads=[pso[bank][:], xold[cb][:]], writes=[xnew[cb][:]])
                    k.dma("a", ohr[:, ot, c0:c0 + w], xnew[cb][:, :w])
                if QP > Q:
                    k.op("v", "memset", zet[:, :QP - Q], 0.0, writes=[zet[:]])
                    k.dma("a", ohr[:, ot, Q:QP], zet[:, :QP - Q])
            k.phase_barrier()


def host_prep_mamba(in_proj_w, conv_weight, conv_bias, dt_bias, A_log, Dv,
                    norm_weight, out_proj_w, core_id):
    ipT = np.ascontiguousarray(np.asarray(in_proj_w, np.float32).T)
    wproj = _tiles_lhsT(ipT, D, 10304)
    cw = np.asarray(conv_weight, np.float32).reshape(48, 128, 4).transpose(1, 0, 2)
    cb = np.asarray(conv_bias, np.float32).reshape(48, 128).T
    woT = np.ascontiguousarray(np.asarray(out_proj_w, np.float32).T)
    wout = _tiles_lhsT(woT, D, D)
    drep = np.zeros((128, 32), np.float32)
    Dv = np.asarray(Dv, np.float32)
    for j in range(32):
        drep[:64, j] = Dv[2 * j]
        drep[64:, j] = Dv[2 * j + 1]
    return {
        "wproj": wproj,
        "convw": np.ascontiguousarray(cw),
        "convb": np.ascontiguousarray(cb),
        "dtb": np.asarray(dt_bias, np.float32).reshape(64, 1),
        "alog": np.asarray(A_log, np.float32).reshape(64, 1),
        "mq0": np.full((64, 1), 0.0 if core_id == 0 else 1.0, np.float32),
        "wout": wout,
        "nw": chan_pd(norm_weight),
        "drep": drep,
    }



RW_TP = 2048


def _rwkv_tables(k, nc, S3, con, W, tag):
    tb = {}
    tdec = S3(f"rw_td{tag}", [128, DT], F32)
    tfp = S3(f"rw_tf{tag}", [128, DT], F32)
    ngt = S3(f"rw_ng{tag}", [128, DT], F32)
    jpk = S3(f"rw_jp{tag}", [128, 256], F32)
    tdT = S3(f"rw_tT{tag}", [64, 64], F32)
    thT = S3(f"rw_hT{tag}", [64, 64], F32)
    k.dma("s", tdec[:], W["tdec_pd"])
    k.dma("s", tfp[:], W["tf_pd"])
    k.dma("s", jpk[:], W["jpack"])
    k.dma("s", tdT[:], W["tdT"])
    k.op("a", "activation", ngt[:], tdec[:], AF.Exp, reads=[tdec[:]], writes=[ngt[:]])
    k.op("v", "tensor_scalar_mul", ngt[:], ngt[:], -1.0, reads=[ngt[:]], writes=[ngt[:]])
    k.op("a", "activation", thT[:], tdT[:], AF.Exp, reads=[tdT[:]], writes=[thT[:]])
    tb["thT"] = thT
    tb["tfp"] = tfp
    spec = [("TtriF", 0, 16), ("TtriG", 16, 16), ("Td1F", 32, 16), ("Td1G", 48, 16),
            ("T32T", 64, 32), ("T32G", 96, 32), ("T64T", 128, 64), ("T64G", 192, 64)]
    tmpf = S3(f"rw_tm{tag}", [128, 64], F32)
    for name, off, wdt in spec:
        t16 = S3(f"rw_{name}{tag}", [128, DT, wdt], BF16)
        for j in range(DT):
            k.op("a", "activation", tmpf[:, :wdt], jpk[:, off:off + wdt], AF.Exp,
                 scale=ngt[:, j:j + 1],
                 reads=[jpk[:], ngt[:]], writes=[tmpf[:]])
            k.op("v", "tensor_scalar_min", t16[:, j, :], tmpf[:, :wdt], 1e18,
                 reads=[tmpf[:]], writes=[t16[:]])
        tb[name] = t16
    for name in ("TtriG", "Td1G", "T32G"):
        wdt = dict(TtriG=16, Td1G=16, T32G=32)[name]
        for par, lo, hi in (("E", 0, 64), ("O", 64, 128)):
            tEO = S3(f"rw_{name}{par}{tag}", [128, DT, wdt], BF16)
            k.op("v", "memset", tEO[:], 0.0, writes=[tEO[:]])
            k.op("v", "tensor_copy", tEO[lo:hi, :, :], tb[name][lo:hi, :, :],
                 reads=[tb[name][:]], writes=[tEO[:]])
            tb[name + par] = tEO
    for par, lo, hi in (("E", 0, 64), ("O", 64, 128)):
        tfEO = S3(f"rw_tf{par}{tag}", [128, DT], F32)
        k.op("v", "memset", tfEO[:], 0.0, writes=[tfEO[:]])
        k.op("v", "tensor_copy", tfEO[lo:hi, :], tfp[lo:hi, :],
             reads=[tfp[:]], writes=[tfEO[:]])
        tb["tfp" + par] = tfEO
    return tb


def ph_rwkv1(k, nc, con, W, x4_hbm, ST2, tag="r1", stages=("mix", "wp", "chain")):
    TP = RW_TP
    Q4 = TP + 1
    with ExitStack() as c2:
        S = lambda n, s, d: c2.enter_context(nc.sbuf_tensor(n, s, d))
        if "mix" in stages and True:
         with ExitStack() as c3:
            S3 = lambda n, s, d: c3.enter_context(nc.sbuf_tensor(n, s, d))
            xn = S3(f"rw_xn{tag}", [128, DT, Q4], BF16)
            ln_chan(k, nc, con, x4_hbm, Q4, xn, tag + "L")
            mq0 = S3(f"rw_mq{tag}", [128, 1], F32)
            mixc = S3(f"rw_mc{tag}", [128, 4 * DT], F32)
            dfb = [S3(f"rw_df{tag}{i}", [128, 512], F32) for i in range(2)]
            mxb = [S3(f"rw_mx{tag}{i}", [128, 512], BF16) for i in range(2)]
            k.dma("s", mq0[:], W["mq0x"])
            k.dma("s", mixc[:], W["mixc"])
            k.op("v", "tensor_scalar_mul", xn[:, :, 0:1].rearrange("p a b -> p (a b)"),
                 xn[:, :, 0:1].rearrange("p a b -> p (a b)"), mq0[:],
                 reads=[xn[:], mq0[:]], writes=[xn[:]])
            for dt_ in range(DT):
                for ci, (c0, w) in enumerate(chunks_of(TP, 512)):
                    b = ci % 2
                    k.op("v", "tensor_sub", dfb[b][:, :w], xn[:, dt_, 1 + c0:1 + c0 + w],
                         xn[:, dt_, c0:c0 + w], reads=[xn[:]], writes=[dfb[b][:]])
                    for mi, mn in enumerate(("mk", "mv", "mr", "mg")):
                        eng = "v"
                        k.op(eng, "scalar_tensor_tensor", mxb[b][:, :w], dfb[b][:, :w],
                             mixc[:, mi * DT + dt_:mi * DT + dt_ + 1],
                             xn[:, dt_, c0:c0 + w], ALU.mult, ALU.add,
                             reads=[dfb[b][:], mixc[:], xn[:]], writes=[mxb[b][:]])
                        k.dma("a", ST2[mn][dt_][:, c0:c0 + w], mxb[b][:, :w])
            k.phase_barrier()
        for wn, mn, outs in ((("wk", "mk", ("kchan", "kT")), ("wv", "mv", ("vT",)),
                             ("wr", "mr", ("rchan",)), ("wg", "mg", ("gg",)))
                             if "wp" in stages else ()):
            with ExitStack() as c3:
                S3 = lambda n, s, d: c3.enter_context(nc.sbuf_tensor(n, s, d))
                PS3 = lambda n, s, d: c3.enter_context(nc.psum_tensor(n, s, d))
                mx = S3(f"rw_m{tag}{wn}", [128, DT, TP], BF16)
                wsl = [S3(f"rw_w{tag}{wn}0", [128, DT, 128], BF16)] * 2
                ob = [S3(f"rw_ob{tag}{wn}0", [128, TP], BF16)] * 2
                stg = S3(f"rw_sg{tag}{wn}", [128, TP], BF16)
                pmm = [PS3(f"rw_p{tag}{wn}{i}", [128, 512], F32) for i in range(8)]
                k.dma("s", mx[:], ST2[mn].rearrange("j p t -> p j t"))
                cks = chunks_of(TP, 512)
                for ot in range(DT):
                    buf = ot % 2
                    k.dma("g", wsl[buf][:], W[wn][ot].rearrange("kt kk m -> kk kt m"))
                    for ci, (c0, w) in enumerate(cks):
                        bank = ((ot % 2) * 4 + ci) % 8
                        gm = k.mm_group()
                        for kt in range(DT):
                            gm.mm(pmm[bank][:, :w], wsl[buf][:, kt, :],
                                  mx[:, kt, c0:c0 + w], last=(kt == DT - 1),
                                  reads=[wsl[buf][:], mx[:]], writes=[pmm[bank][:]])
                        k.op("a", "copy", ob[buf][:, c0:c0 + w], pmm[bank][:, :w],
                             reads=[pmm[bank][:]], writes=[ob[buf][:]])
                    if "kchan" in outs:
                        k.dma("a", ST2["kchan"][ot], ob[buf][:])
                    if "rchan" in outs:
                        k.dma("a", ST2["rchan"][ot], ob[buf][:])
                    if "gg" in outs:
                        k.dma("a", ST2["gg"][2 * ot], ob[buf][0:64, :])
                        k.dma("a", ST2["gg"][2 * ot + 1], ob[buf][64:128, :])
                    if "kT" in outs or "vT" in outs:
                        dst = ST2["kT"] if "kT" in outs else ST2["vT"]
                        for qb in range(TP // 128):
                            k.dma("s", stg[:, qb * 128:(qb + 1) * 128],
                                  ob[buf][:, qb * 128:(qb + 1) * 128], transpose=True)
                        k.dma("a", dst[:, :, ot, :].rearrange("qb q pin -> q qb pin"),
                              stg[:].rearrange("q (qb pin) -> q qb pin", pin=128))
                k.phase_barrier()
        if "chain" not in stages:
            return
        with ExitStack() as c3:
            S3 = lambda n, s, d: c3.enter_context(nc.sbuf_tensor(n, s, d))
            PS3 = lambda n, s, d: c3.enter_context(nc.psum_tensor(n, s, d))
            tb = _rwkv_tables(k, nc, S3, con, W, tag)
            t64gt = S3(f"rw_gT{tag}", [64, 4096], BF16)
            ptp = PS3(f"rw_pt{tag}", [64, 128], BF16)
            for j in range(DT):
                k.op("t", "transpose", ptp[:, :], tb["T64G"][:, j, :], con.id128b[:],
                     reads=[tb["T64G"][:], con.id128b[:]], writes=[ptp[:]])
                k.op("a", "copy", t64gt[:, j * 128:(j + 1) * 128], ptp[:, :],
                     reads=[ptp[:]], writes=[t64gt[:]])
            dec64 = S3(f"rw_d64{tag}", [64, 64], F32)
            k.op("a", "activation", dec64[:], tb["thT"][:], AF.Exp, scale=-64.0,
                 reads=[tb["thT"][:]], writes=[dec64[:]])
            ktl = [S3(f"rw_ktl{tag}0", [64, DT, 128], BF16)] * 2
            vtl = [S3(f"rw_vtl{tag}0", [64, DT, 128], BF16)] * 2
            khat = S3(f"rw_kh{tag}", [64, 4096], BF16)
            ub = S3(f"rw_ub{tag}", [64, 4096], F32)
            ss = S3(f"rw_ss{tag}", [64, 4096], F32)
            s16 = S3(f"rw_s6{tag}", [64, 4096], BF16)
            pu = [PS3(f"rw_pu{tag}0", [64, 2048], F32)] * 2
            k.op("v", "memset", ss[:], 0.0, writes=[ss[:]])
            for c in range(TP // 64):
                buf = c % 2
                qb, hf = c // 2, (c % 2) * 64
                k.op("v", "tensor_copy", s16[:], ss[:], reads=[ss[:]], writes=[s16[:]])
                k.dma("a", ST2["st"][c], s16[:])
                k.dma("s", ktl[buf][:], ST2["kT"][qb, hf:hf + 64])
                k.dma("s", vtl[buf][0:64, :, :], ST2["vT"][qb, hf:hf + 64])
                k.op("v", "tensor_mul", khat[:],
                     ktl[buf][:].rearrange("q j p -> q (j p)"), t64gt[:],
                     reads=[ktl[buf][:], t64gt[:]], writes=[khat[:]])
                vtf = vtl[buf][:].rearrange("q j p -> q (j p)")
                for half in range(2):
                    for hh in range(32):
                        h = half * 32 + hh
                        gmm = k.mm_group()
                        gmm.mm(pu[half][:, hh * 64:(hh + 1) * 64],
                               khat[:, h * 64:(h + 1) * 64],
                               vtf[:, h * 64:(h + 1) * 64], last=True,
                               reads=[khat[:], vtl[buf][:]], writes=[pu[half][:]])
                    k.op("v", "tensor_copy", ub[:, half * 2048:(half + 1) * 2048],
                         pu[half][:], reads=[pu[half][:]], writes=[ub[:]])
                k.op("v", "tensor_tensor",
                     ss[:].rearrange("p (h e) -> p h e", e=64),
                     ss[:].rearrange("p (h e) -> p h e", e=64),
                     dec64[:].rearrange("p (h o) -> p h o", o=1).broadcast_to((64, 64, 64)),
                     ALU.mult, reads=[ss[:], dec64[:]], writes=[ss[:]])
                k.op("v", "tensor_add", ss[:], ss[:], ub[:],
                     reads=[ss[:], ub[:]], writes=[ss[:]])
            k.dma("a", ST2["utot"], ss[:])
            k.phase_barrier()


def ph_rwkv2(k, nc, con, W, ST2, SG2, x4_hbm, x5_hbm, tag="r2",
             stages=("fold", "amm", "merge", "omm", "norm", "wo")):
    TP = RW_TP
    NCH = TP // 64
    with ExitStack() as c2:
        S = lambda n, s, d: c2.enter_context(nc.sbuf_tensor(n, s, d))
        PS = lambda n, s, d: c2.enter_context(nc.psum_tensor(n, s, d))
        sin = S(f"r2_si{tag}", [64, 4096], F32)
        with ExitStack() as c3:
            S3 = lambda n, s, d: c3.enter_context(nc.sbuf_tensor(n, s, d))
            PS3 = lambda n, s, d: c3.enter_context(nc.psum_tensor(n, s, d))
            tb = _rwkv_tables(k, nc, S3, con, W, tag)
            with ExitStack() as c3b:
                S3b = lambda n, s, d: c3b.enter_context(nc.sbuf_tensor(n, s, d))
                fm64 = S3b(f"r2_fm{tag}", [64, 8], F32)
                ifm64 = S3b(f"r2_if{tag}", [64, 8], F32)
                lamf = S3b(f"r2_lf{tag}", [64, 64], F32)
                cj = S3b(f"r2_cj{tag}", [64, 64], F32)
                uj = S3b(f"r2_uj{tag}", [64, 4096], F32)
                k.dma("s", fm64[:], SG2["fm64"])
                k.dma("s", ifm64[:], SG2["ifm64"])
                k.op("a", "activation", lamf[:], tb["thT"][:], AF.Exp, scale=-float(TP),
                     reads=[tb["thT"][:]], writes=[lamf[:]])
                k.op("v", "memset", sin[:], 0.0, writes=[sin[:]])
                for j in range(8):
                    k.op("v", "tensor_scalar", cj[:], lamf[:], fm64[:, j:j + 1],
                         ifm64[:, j:j + 1], ALU.mult, ALU.add,
                         reads=[lamf[:], fm64[:], ifm64[:]], writes=[cj[:]])
                    k.op("v", "tensor_tensor",
                         sin[:].rearrange("p (h e) -> p h e", e=64),
                         sin[:].rearrange("p (h e) -> p h e", e=64),
                         cj[:].rearrange("p (h o) -> p h o", o=1).broadcast_to((64, 64, 64)),
                         ALU.mult, reads=[sin[:], cj[:]], writes=[sin[:]])
                    k.dma("s", uj[:], SG2["ut"][j])
                    k.op("v", "scalar_tensor_tensor", sin[:], uj[:], fm64[:, j:j + 1],
                         sin[:], ALU.mult, ALU.add,
                         reads=[uj[:], fm64[:], sin[:]], writes=[sin[:]])
                k.phase_barrier()
            masks = S3(f"r2_mk{tag}", [64, 4, 512], BF16)
            k.dma("g", masks[:], W["amasks"])
            mdiag8 = S3(f"r2_m8{tag}", [64, 512], mybir.dt.uint8)
            k.dma("s", mdiag8[:], W["amasku8"])
            gnw = S3(f"r2_gw{tag}", [64, 64], F32)
            gnb = S3(f"r2_gb{tag}", [64, 64], F32)
            k.dma("s", gnw[:], W["gnw_g"])
            k.dma("s", gnb[:], W["gnb_g"])
            kcl = [S3(f"r2_kc{tag}0", [128, DT, 64], BF16)] * 2
            rcl = [S3(f"r2_rc{tag}0", [128, DT, 64], BF16)] * 2
            vtl = [S3(f"r2_vt{tag}0", [128, DT, 128], BF16)] * 2
            stl = [S3(f"r2_st{tag}0", [64, 4096], BF16)] * 2
            dec = S3(f"r2_dc{tag}", [64, 64], F32)
            scbE = S3(f"r2_sbE{tag}", [128, 4096], BF16)
            scbO = S3(f"r2_sbO{tag}", [128, 4096], BF16)
            k.op("v", "memset", scbE[:], 0.0, writes=[scbE[:]])
            k.op("v", "memset", scbO[:], 0.0, writes=[scbO[:]])
            vnt = {}
            for vn in ("ktriE", "ktriO", "rtri", "kd1E", "kd1O", "rd1",
                       "k32E", "k32O", "r32", "r64", "ktfE", "ktfO"):
                vnt[vn] = S3(f"r2_{vn}{tag}", [128, DT, 64], BF16)
            ea = [S3(f"r2_ea{tag}{i}", [64, 512], BF16) for i in range(4)]
            g1b = S3(f"r2_g1{tag}", [64, 512], BF16)
            g2b = S3(f"r2_g2{tag}", [64, 512], BF16)
            g3b = S3(f"r2_g3{tag}", [64, 512], BF16)
            ahb = S3(f"r2_ah{tag}", [128, 512], BF16)
            ybg = S3(f"r2_yb{tag}", [64, 64, 64], BF16)
            ysq = S3(f"r2_yq{tag}", [64, 4096], BF16)
            qrow = S3(f"r2_qr{tag}", [1, 4096], BF16)
            mrow = S3(f"r2_mr{tag}", [1, 4096], BF16)
            t5 = S3(f"r2_t5{tag}", [64, 4096], BF16)
            ynb = S3(f"r2_yn{tag}", [64, 4096], BF16)
            oneb = S3(f"r2_ob{tag}", [64, 1], BF16)
            k.op("v", "memset", oneb[:], 1.0, writes=[oneb[:]])
            k.op("v", "memset", vtl[0][64:128, :, :], 0.0, writes=[vtl[0][:]])
            k.op("v", "memset", ahb[64:128, :], 0.0, writes=[ahb[:]])
            pA = [PS3(f"r2_pA{tag}{i}", [64, 512], F32) for i in range(4)]
            pO = [PS3(f"r2_pO{tag}{i}", [64, 512], F32) for i in range(2)]
            pSt = PS3(f"r2_pS{tag}", [1, 512], F32)
            pBc = PS3(f"r2_pB{tag}", [64, 512], F32)
            kcr = ST2["kchan"].rearrange("j p t -> p j t")
            rcr = ST2["rchan"].rearrange("j p t -> p j t")
            for c in range(NCH if "amm" in stages or "merge" in stages else 0):
                buf = c % 2
                c0 = c * 64
                qb, hf = c // 2, (c % 2) * 64
                k.dma("s", kcl[buf][:], kcr[:, :, c0:c0 + 64])
                k.dma("s", rcl[buf][:], rcr[:, :, c0:c0 + 64])
                k.dma("s", vtl[buf][0:64, :, :], ST2["vT"][qb, hf:hf + 64])
                k.dma("s", stl[buf][:], ST2["st"][c])
                k.op("a", "activation", dec[:], tb["thT"][:], AF.Exp,
                     scale=-64.0 * c, reads=[tb["thT"][:]], writes=[dec[:]])
                k.op("v", "tensor_tensor",
                     scbE[0:64, :].rearrange("p (h e) -> p h e", e=64),
                     sin[:].rearrange("p (h e) -> p h e", e=64),
                     dec[:].rearrange("p (h o) -> p h o", o=1).broadcast_to((64, 64, 64)),
                     ALU.mult, reads=[sin[:], dec[:]], writes=[scbE[:]])
                k.op("v", "tensor_add", scbE[0:64, :], scbE[0:64, :], stl[buf][:],
                     reads=[scbE[:], stl[buf][:]], writes=[scbE[:]])
                k.op("a", "copy", scbO[64:128, :], scbE[0:64, :],
                     reads=[scbE[:]], writes=[scbO[:]])
                kv = kcl[buf][:]
                rv = rcl[buf][:]
                for vn, src, tbl, per in (("ktriE", kv, "TtriGE", 16), ("ktriO", kv, "TtriGO", 16),
                                          ("rtri", rv, "TtriF", 16),
                                          ("kd1E", kv, "Td1GE", 16), ("kd1O", kv, "Td1GO", 16),
                                          ("rd1", rv, "Td1F", 16),
                                          ("k32E", kv, "T32GE", 32), ("k32O", kv, "T32GO", 32),
                                          ("r32", rv, "T32T", 32),
                                          ("r64", rv, "T64T", 64)):
                    eng = "v" if vn[0] == "k" else "g"
                    rep = 64 // per
                    k.op(eng, "tensor_tensor",
                         vnt[vn][:].rearrange("p j (r q) -> p j r q", q=per),
                         src.rearrange("p j (r q) -> p j r q", q=per),
                         tb[tbl][:].unsqueeze(2).broadcast_to((128, DT, rep, per)),
                         ALU.mult, reads=[kcl[buf][:], rcl[buf][:], tb[tbl][:]],
                         writes=[vnt[vn][:]])
                for j in range(DT):
                    k.op("v", "tensor_scalar_mul", vnt["ktfE"][:, j, :],
                         kcl[buf][:, j, :], tb["tfpE"][:, j:j + 1],
                         reads=[kcl[buf][:], tb["tfpE"][:]], writes=[vnt["ktfE"][:]])
                    k.op("v", "tensor_scalar_mul", vnt["ktfO"][:, j, :],
                         kcl[buf][:, j, :], tb["tfpO"][:, j:j + 1],
                         reads=[kcl[buf][:], tb["tfpO"][:]], writes=[vnt["ktfO"][:]])
                vts = vtl[buf][:].rearrange("q j p -> q (j p)")
                for gi in range(8):
                    for hh in range(8):
                        h = gi * 8 + hh
                        j = h // 2
                        par = "E" if h % 2 == 0 else "O"
                        csl = slice(hh * 64, (hh + 1) * 64)
                        for pi, (lv, rvn) in enumerate(
                                ((("ktri", "rtri"), ("kd1", "rd1"), ("k32", "r32"))
                                 if "amm" in stages else ())):
                            gx = k.mm_group()
                            gx.mm(pA[pi][:, csl], vnt[lv + par][:, j, :],
                                  vnt[rvn][:, j, :], last=True,
                                  reads=[vnt[lv + par][:], vnt[rvn][:]],
                                  writes=[pA[pi][:]])
                        if "amm" in stages:
                            gx = k.mm_group()
                            gx.mm(pA[3][:, csl], vnt["ktf" + par][:, j, :],
                                  rcl[buf][:, j, :], last=True,
                                  reads=[vnt["ktf" + par][:], rcl[buf][:]],
                                  writes=[pA[3][:]])
                    if "merge" not in stages:
                        continue
                    for i in range(4):
                        k.op("a", "copy", ea[i][:], pA[i][:],
                             reads=[pA[i][:]], writes=[ea[i][:]])
                    k.op("g", "tensor_mul", g1b[:], ea[0][:], masks[:, 0, :],
                         reads=[ea[0][:], masks[:]], writes=[g1b[:]])
                    k.op("g", "tensor_mul", g2b[:], ea[1][:], masks[:, 1, :],
                         reads=[ea[1][:], masks[:]], writes=[g2b[:]])
                    k.op("v", "tensor_mul", g3b[:], ea[2][:], masks[:, 2, :],
                         reads=[ea[2][:], masks[:]], writes=[g3b[:]])
                    k.op("g", "tensor_add", g1b[:], g1b[:], g2b[:],
                         reads=[g1b[:], g2b[:]], writes=[g1b[:]])
                    k.op("v", "tensor_add", ahb[0:64, :], g1b[:], g3b[:],
                         reads=[g1b[:], g3b[:]], writes=[ahb[:]])
                    k.op("v", "copy_predicated", ahb[0:64, :], mdiag8[:], ea[3][:],
                         reads=[mdiag8[:], ea[3][:]], writes=[ahb[:]])
                    if "omm" not in stages:
                        continue
                    for hh in range(8):
                        h = gi * 8 + hh
                        j = h // 2
                        scbp = scbE if h % 2 == 0 else scbO
                        csl = slice(hh * 64, (hh + 1) * 64)
                        go = k.mm_group()
                        go.mm(pO[gi % 2][:, csl], vts[:, h * 64:(h + 1) * 64],
                              ahb[:, csl], last=False,
                              reads=[vtl[buf][:], ahb[:]], writes=[pO[gi % 2][:]])
                        go.mm(pO[gi % 2][:, csl],
                              scbp[:, h * 64:(h + 1) * 64],
                              vnt["r64"][:, j, :],
                              last=True, reads=[scbp[:], vnt["r64"][:]],
                              writes=[pO[gi % 2][:]])
                    k.op("v", "tensor_copy",
                         ybg[:, gi * 8:(gi + 1) * 8, :].rearrange("e h t -> e (h t)"),
                         pO[gi % 2][:], reads=[pO[gi % 2][:]], writes=[ybg[:]])
                if "norm" not in stages:
                    continue
                ybf = ybg[:].rearrange("e h t -> e (h t)")
                for bcq in range(8):
                    sl_ = slice(bcq * 512, (bcq + 1) * 512)
                    gs_ = k.mm_group()
                    gs_.mm(pSt[:], oneb[:], ybf[:, sl_], last=True,
                           reads=[oneb[:], ybg[:]], writes=[pSt[:]])
                    k.op("a", "activation", mrow[:, sl_], pSt[:], AF.Copy,
                         scale=1.0 / 64, reads=[pSt[:]], writes=[mrow[:]])
                    gbq = k.mm_group()
                    gbq.mm(pBc[:, :], oneb[0:1, :].broadcast_to((1, 64)), mrow[:, sl_],
                           last=True, reads=[oneb[:], mrow[:]], writes=[pBc[:]])
                    k.op("v", "tensor_sub", t5[:, sl_], ybf[:, sl_], pBc[:, :],
                         reads=[ybg[:], pBc[:]], writes=[t5[:]])
                k.op("a", "activation", ysq[:], t5[:], AF.Square,
                     reads=[t5[:]], writes=[ysq[:]])
                for bcq in range(8):
                    sl_ = slice(bcq * 512, (bcq + 1) * 512)
                    gq_ = k.mm_group()
                    gq_.mm(pSt[:], oneb[:], ysq[:, sl_], last=True,
                           reads=[oneb[:], ysq[:]], writes=[pSt[:]])
                    k.op("a", "activation", qrow[:, sl_], pSt[:], AF.Copy,
                         scale=1.0 / 64, reads=[pSt[:]], writes=[qrow[:]])
                k.op("v", "tensor_scalar_add", qrow[:], qrow[:], EPS,
                     reads=[qrow[:]], writes=[qrow[:]])
                k.op("a", "activation", qrow[:], qrow[:], AF.Ln,
                     reads=[qrow[:]], writes=[qrow[:]])
                k.op("a", "activation", qrow[:], qrow[:], AF.Exp, scale=-0.5,
                     reads=[qrow[:]], writes=[qrow[:]])
                for bcq in range(8):
                    sl_ = slice(bcq * 512, (bcq + 1) * 512)
                    gbq2 = k.mm_group()
                    gbq2.mm(pBc[:, :], oneb[0:1, :].broadcast_to((1, 64)), qrow[:, sl_],
                            last=True, reads=[oneb[:], qrow[:]], writes=[pBc[:]])
                    k.op("v", "tensor_mul", t5[:, sl_], t5[:, sl_], pBc[:, :],
                         reads=[t5[:], pBc[:]], writes=[t5[:]])
                k.op("g", "tensor_tensor",
                     t5[:].rearrange("e (h t) -> e h t", t=64),
                     t5[:].rearrange("e (h t) -> e h t", t=64),
                     gnw[:].unsqueeze(2).broadcast_to((64, 64, 64)), ALU.mult,
                     reads=[t5[:], gnw[:]], writes=[t5[:]])
                k.op("g", "tensor_tensor",
                     t5[:].rearrange("e (h t) -> e h t", t=64),
                     t5[:].rearrange("e (h t) -> e h t", t=64),
                     gnb[:].unsqueeze(2).broadcast_to((64, 64, 64)), ALU.add,
                     reads=[t5[:], gnb[:]], writes=[t5[:]])
                k.dma("s", ynb[:].rearrange("e (h t) -> e h t", t=64),
                      ST2["gg"][:, :, c0:c0 + 64].rearrange("h e t -> e h t"))
                k.op("a", "activation", ynb[:], ynb[:], AF.Silu,
                     reads=[ynb[:]], writes=[ynb[:]])
                k.op("v", "tensor_tensor", ynb[:], ynb[:], t5[:], ALU.mult,
                     reads=[ynb[:], t5[:]], writes=[ynb[:]])
                k.dma("a", ST2["yn"].rearrange("j (hf e) t -> e j hf t", hf=2)
                      [:, :, :, c0:c0 + 64],
                      ynb[:].rearrange("e (j hf t) -> e j hf t", hf=2, t=64))
            k.phase_barrier()
        if "wo" not in stages:
            return
        with ExitStack() as c5:
            S5 = lambda n, s, d: c5.enter_context(nc.sbuf_tensor(n, s, d))
            PS5 = lambda n, s, d: c5.enter_context(nc.psum_tensor(n, s, d))
            wsl = [S5(f"r2_w{tag}0", [128, DT, 128], BF16)] * 2
            ynl = [S5(f"r2_yl{tag}{i}", [128, DT, 512], BF16) for i in range(2)]
            xold = [S5(f"r2_xo{tag}{i}", [128, 512], F32) for i in range(2)]
            xnew = [S5(f"r2_xw{tag}{i}", [128, 512], F32) for i in range(2)]
            pso = [PS5(f"r2_p{tag}{i}", [128, 512], F32) for i in range(8)]
            cks = chunks_of(TP, 512)
            xhr = x4_hbm.rearrange("dt p t -> p dt t")
            ohr = x5_hbm.rearrange("dt p t -> p dt t")
            ynr2 = ST2["yn"].rearrange("j p t -> p j t")
            for ot in range(DT):
                buf = ot % 2
                k.dma("g", wsl[buf][:], W["wo"][ot].rearrange("kt kk m -> kk kt m"))
                for ci, (c0, w) in enumerate(cks):
                    cb = ci % 2
                    k.dma("s", ynl[cb][:, :, :w], ynr2[:, :, c0:c0 + w])
                    bank = ((ot % 2) * 4 + ci) % 8
                    gm = k.mm_group()
                    for kt in range(DT):
                        gm.mm(pso[bank][:, :w], wsl[buf][:, kt, :],
                              ynl[cb][:, kt, :w], last=(kt == DT - 1),
                              reads=[wsl[buf][:], ynl[cb][:]], writes=[pso[bank][:]])
                    k.dma("s", xold[cb][:, :w], xhr[:, ot, 1 + c0:1 + c0 + w])
                    k.op("v", "tensor_add", xnew[cb][:, :w], pso[bank][:, :w],
                         xold[cb][:, :w],
                         reads=[pso[bank][:], xold[cb][:]], writes=[xnew[cb][:]])
                    k.dma("a", ohr[:, ot, c0:c0 + w], xnew[cb][:, :w])
            k.phase_barrier()


def host_prep_rwkv(w_key, w_value, w_receptance, w_gate, w_output,
                   tmk, tmv, tmr, tmg, time_decay, time_faaaa, gnw, gnb, core_id):
    f = np.float32
    mk = lambda w: _tiles_lhsT(np.ascontiguousarray(np.asarray(w, f).T), D, D)
    mixc = np.concatenate([chan_pd(tmk), chan_pd(tmv), chan_pd(tmr), chan_pd(tmg)],
                          axis=1)
    td = np.asarray(time_decay, f)
    tf_ = np.asarray(time_faaaa, f)
    jpack = np.zeros((128, 256), f)
    jpack[:, 0:16] = np.arange(16) - 8
    jpack[:, 16:32] = 7 - np.arange(16)
    jpack[:, 32:48] = np.arange(16)
    jpack[:, 48:64] = 15 - np.arange(16)
    jpack[:, 64:96] = np.arange(32)
    jpack[:, 96:128] = 31 - np.arange(32)
    jpack[:, 128:192] = np.arange(64)
    jpack[:, 192:256] = 63 - np.arange(64)
    mtri = np.zeros((64, 64), f)
    md1 = np.zeros((64, 64), f)
    mfar = np.zeros((64, 64), f)
    mdg = np.zeros((64, 64), f)
    for si in range(64):
        for ti in range(64):
            if si // 16 == ti // 16 and si < ti:
                mtri[si, ti] = 1
            if ti // 16 - si // 16 == 1:
                md1[si, ti] = 1
            if si // 32 == 0 and ti // 32 == 1 and ti // 16 - si // 16 >= 2:
                mfar[si, ti] = 1
            if si == ti:
                mdg[si, ti] = 1
    amasks = np.stack([np.tile(mtri, (1, 8)), np.tile(md1, (1, 8)),
                       np.tile(mfar, (1, 8)), np.tile(mdg, (1, 8))], axis=1)
    amasku8 = np.ascontiguousarray(np.tile(mdg, (1, 8)).astype(np.uint8))
    fm = np.zeros((64, 8), f)
    fm[:, :core_id] = 1.0
    return {
        "wk": mk(w_key), "wv": mk(w_value), "wr": mk(w_receptance),
        "wg": mk(w_gate), "wo": mk(w_output),
        "mixc": np.ascontiguousarray(mixc),
        "tdec_pd": chan_pd(td.reshape(-1)),
        "tdT": np.ascontiguousarray(td.T),
        "tf_pd": chan_pd(tf_.reshape(-1)),
        "jpack": jpack,
        "amasks": amasks,
        "amasku8": amasku8,
        "gnw_g": np.ascontiguousarray(np.asarray(gnw, f).reshape(64, 64).T),
        "gnb_g": np.ascontiguousarray(np.asarray(gnb, f).reshape(64, 64).T),
        "mq0x": np.full((128, 1), 0.0 if core_id == 0 else 1.0, f),
        "fm64": fm,
        "ifm64": 1.0 - fm,
    }



TA_ = T + 4
QM = T + 1
QMP = 2176
NCHM = (QM - 1) // 128


def _stash_decls(nc, kind):
    dr = lambda n, s, d: nc.dram_tensor(n, s, d, kind=kind).ap()
    ST = {"gate": dr("st_gate", [32, 128, QMP], BF16),
          "xs": dr("st_xs", [32, 128, QMP], BF16),
          "dtxT": dr("st_dtxT", [QMP // 128, 128, 32, 128], BF16),
          "bb": dr("st_bb", [8, 128, QMP], BF16),
          "cc": dr("st_cc", [8, 128, QMP], BF16),
          "bbT": dr("st_bbT", [QMP // 128, 128, 8, 128], BF16),
          "at": dr("st_at", [64, QMP], F32),
          "dpe": dr("st_dpe", [64, NCHM + 1], F32),
          "lamtot": dr("st_lamtot", [64, 1], F32),
          "st": dr("st_st", [NCHM + 1, 128, 4096], BF16),
          "utot": dr("st_utot", [128, 4096], F32),
          "yn": dr("st_yn", [32, 128, QMP], BF16)}
    return ST


def _stash2_decls(nc, kind):
    dr = lambda n, s, d: nc.dram_tensor(n, s, d, kind=kind).ap()
    TP = RW_TP
    ST2 = {n: dr("s2_" + n, s, d) for n, (s, d) in {
        "mk": ([32, 128, TP], BF16), "mv": ([32, 128, TP], BF16),
        "mr": ([32, 128, TP], BF16), "mg": ([32, 128, TP], BF16),
        "kchan": ([32, 128, TP], BF16), "rchan": ([32, 128, TP], BF16),
        "kT": ([TP // 128, 128, 32, 128], BF16),
        "vT": ([TP // 128, 128, 32, 128], BF16),
        "gg": ([64, 64, TP], BF16),
        "st": ([TP // 64, 64, 4096], BF16), "utot": ([64, 4096], F32),
        "yn": ([32, 128, TP], BF16)}.items()}
    return ST2


def _w_decl(nc, names_shapes, dt=F32):
    return {n: nc.dram_tensor("w_" + n, s, dt, kind="ExternalInput").ap()
            for n, s in names_shapes.items()}


W0N = {"a": [8, 32, 128, 128], "b": [32, 8, 128, 128], "wr": [32, 128, 64]}
WMN = {"wproj": [81, 32, 128, 128], "convw": [128, 48, 4], "convb": [128, 48],
       "dtb": [64, 1], "alog": [64, 1], "mq0": [64, 1],
       "wout": [32, 32, 128, 128], "nw": [128, 32], "drep": [128, 32]}
W1N = {"w": [32, 32, 128, 128], "bias": [128, 32], "scale": [1]}
WRN = {"wk": [32, 32, 128, 128], "wv": [32, 32, 128, 128], "wr": [32, 32, 128, 128],
       "wg": [32, 32, 128, 128], "wo": [32, 32, 128, 128],
       "mixc": [128, 128], "tdec_pd": [128, 32], "tdT": [64, 64],
       "tf_pd": [128, 32], "jpack": [128, 256], "amasks": [64, 4, 512],
       "gnw_g": [64, 64], "gnb_g": [64, 64], "mq0x": [128, 1],
       "fm64": [64, 8], "ifm64": [64, 8]}


def build_L1():
    nc = bass.Bass("TRN2", target_bir_lowering=False)
    x0 = nc.dram_tensor("x0", [TA_, D], F32, kind="ExternalInput").ap()
    x1 = nc.dram_tensor("x1", [DT, 128, pad128(TA_)], F32, kind="ExternalOutput").ap()
    W0 = _w_decl(nc, {"f0" + n: s for n, s in W0N.items()})
    W0 = {n: W0["f0" + n] for n in W0N}
    WM = _w_decl(nc, {"m" + n: s for n, s in WMN.items()})
    WM = {n: WM["m" + n] for n in WMN}
    ST = _stash_decls(nc, "ExternalOutput")
    hgs = nc.dram_tensor("hgs", [8, 128, pad128(TA_)], BF16, kind="Internal").ap()
    ctx = ExitStack()
    k = Kit(nc, ctx)
    con = Consts(nc, ctx, k)
    x0c = nc.dram_tensor("x0c", [DT, 128, pad128(TA_)], F32, kind="Internal").ap()
    ph_input_tokmajor(k, nc, con, x0, TA_, x0c)
    ph_ffn0(k, nc, con, W0, x0c, TA_, x1, "A", hg_hbm=hgs)
    ph_mamba1(k, nc, con, WM, x1, QM, ST)
    ctx.close()
    return nc


def build_L2():
    nc = bass.Bass("TRN2", target_bir_lowering=False)
    x1 = nc.dram_tensor("x1", [DT, 128, pad128(TA_)], F32, kind="ExternalInput").ap()
    x4 = nc.dram_tensor("x4", [DT, 128, pad128(QM)], F32, kind="ExternalOutput").ap()
    WM = _w_decl(nc, {"m" + n: s for n, s in WMN.items()})
    WM = {n: WM["m" + n] for n in WMN}
    W0 = _w_decl(nc, {"f0" + n: s for n, s in W0N.items()})
    W0 = {n: W0["f0" + n] for n in W0N}
    W1 = _w_decl(nc, {"f1" + n: s for n, s in W1N.items()})
    W1 = {n: W1["f1" + n] for n in W1N}
    WR = _w_decl(nc, {"r" + n: s for n, s in WRN.items()})
    WR = {n: WR["r" + n] for n in WRN}
    WR["amasku8"] = nc.dram_tensor("w_amasku8", [64, 512], mybir.dt.uint8,
                                   kind="ExternalInput").ap()
    ST = _stash_decls(nc, "ExternalInput")
    ST2 = _stash2_decls(nc, "ExternalOutput")
    SG = {"ut": nc.dram_tensor("sg_ut", [8, 128, 4096], F32, kind="ExternalInput").ap(),
          "lam": nc.dram_tensor("sg_lam", [64, 8], F32, kind="ExternalInput").ap(),
          "fmc": nc.dram_tensor("sg_fmc", [64, 8], F32, kind="ExternalInput").ap(),
          "fmc128": nc.dram_tensor("sg_fmc128", [128, 8], F32, kind="ExternalInput").ap()}
    x2 = nc.dram_tensor("x2", [DT, 128, pad128(QM)], F32, kind="Internal").ap()
    x3 = nc.dram_tensor("x3", [DT, 128, pad128(QM)], F32, kind="Internal").ap()
    hgs = nc.dram_tensor("hgs", [8, 128, pad128(QM)], BF16, kind="Internal").ap()
    ctx = ExitStack()
    k = Kit(nc, ctx)
    con = Consts(nc, ctx, k)
    ph_mamba2(k, nc, con, WM, ST, SG, x1, QM, x2)
    ph_ffn0(k, nc, con, W0, x2, QM, x3, "C", hg_hbm=hgs)
    ph_ffn1(k, nc, con, W1, x3, QM, x4, "D")
    ph_rwkv1(k, nc, con, WR, x4, ST2)
    ctx.close()
    return nc


def build_L3():
    nc = bass.Bass("TRN2", target_bir_lowering=False)
    x4 = nc.dram_tensor("x4", [DT, 128, pad128(QM)], F32, kind="ExternalInput").ap()
    out = nc.dram_tensor("out", [T, D], F32, kind="ExternalOutput").ap()
    W1 = _w_decl(nc, {"f1" + n: s for n, s in W1N.items()})
    W1 = {n: W1["f1" + n] for n in W1N}
    WR = _w_decl(nc, {"r" + n: s for n, s in WRN.items()})
    WR = {n: WR["r" + n] for n in WRN}
    WR["amasku8"] = nc.dram_tensor("w_amasku8", [64, 512], mybir.dt.uint8,
                                   kind="ExternalInput").ap()
    ST2 = _stash2_decls(nc, "ExternalInput")
    SG2 = {"ut": nc.dram_tensor("g2_ut", [8, 64, 4096], F32, kind="ExternalInput").ap(),
           "fm64": nc.dram_tensor("g2_fm64", [64, 8], F32, kind="ExternalInput").ap(),
           "ifm64": nc.dram_tensor("g2_ifm64", [64, 8], F32, kind="ExternalInput").ap()}
    x5 = nc.dram_tensor("x5", [DT, 128, RW_TP], F32, kind="Internal").ap()
    x6 = nc.dram_tensor("x6", [DT, 128, RW_TP], F32, kind="Internal").ap()
    ctx = ExitStack()
    k = Kit(nc, ctx)
    con = Consts(nc, ctx, k)
    ph_rwkv2(k, nc, con, WR, ST2, SG2, x4, x5)
    ph_ffn1(k, nc, con, W1, x5, RW_TP, x6, "F")
    ph_output_tokmajor(k, nc, con, x6, T, out)
    ctx.close()
    return nc


_BUILD_CACHE = {}


def kernel_unfused(**inputs):
    f = np.float32
    xfull = np.asarray(inputs["L0_attn_hidden_states"], f).reshape(16384, D)
    W0h = host_prep_ffn0(inputs["L0_ffn_A"], inputs["L0_ffn_B"], inputs["L0_ffn_Wr"])
    W1h = host_prep_ffn1(inputs["L1_ffn_weight"], inputs["L1_ffn_weight_scale"],
                         inputs["L1_ffn_bias"])
    conh = Consts.host_inputs()

    def wm_h(cid):
        return host_prep_mamba(inputs["L0_attn_in_proj_w"], inputs["L0_attn_conv_weight"],
                               inputs["L0_attn_conv_bias"], inputs["L0_attn_dt_bias"],
                               inputs["L0_attn_A_log"], inputs["L0_attn_D"],
                               inputs["L0_attn_norm_weight"], inputs["L0_attn_out_proj_w"],
                               cid)

    def wr_h(cid):
        return host_prep_rwkv(inputs["L1_attn_w_key"], inputs["L1_attn_w_value"],
                              inputs["L1_attn_w_receptance"], inputs["L1_attn_w_gate"],
                              inputs["L1_attn_w_output"], inputs["L1_attn_time_mix_key"],
                              inputs["L1_attn_time_mix_value"],
                              inputs["L1_attn_time_mix_receptance"],
                              inputs["L1_attn_time_mix_gate"], inputs["L1_attn_time_decay"],
                              inputs["L1_attn_time_faaaa"], inputs["L1_attn_gn_weight"],
                              inputs["L1_attn_gn_bias"], cid)

    WMh = wm_h(1)
    WRh = wr_h(1)
    if "L1" not in _BUILD_CACHE:
        _BUILD_CACHE["L1"] = build_L1()
    in_maps = []
    for c in range(NCORES):
        im = dict(conh)
        x0s = np.zeros((TA_, D), f)
        lo = c * T - 4
        src_lo = max(lo, 0)
        x0s[src_lo - lo:, :] = xfull[src_lo:c * T + T]
        im["x0"] = x0s
        for n in W0N:
            im["w_f0" + n] = W0h[n]
        for n in WMN:
            im["w_m" + n] = WMh[n] if n != "mq0" else np.full(
                (64, 1), 0.0 if c == 0 else 1.0, f)
        in_maps.append(im)
    r1 = run_bass_kernel_spmd(_BUILD_CACHE["L1"], in_maps, core_ids=list(range(NCORES)))
    lam8 = np.stack([np.asarray(r1.results[c]["st_lamtot"])[:, 0]
                     for c in range(NCORES)], axis=1).astype(f)
    ut8 = np.stack([np.asarray(r1.results[c]["st_utot"]) for c in range(NCORES)]).astype(f)
    if "L2" not in _BUILD_CACHE:
        _BUILD_CACHE["L2"] = build_L2()
    in_maps2 = []
    for c in range(NCORES):
        im = dict(conh)
        im["x1"] = np.asarray(r1.results[c]["x1"])
        for n in ("gate", "xs", "dtxT", "bb", "cc", "bbT", "at", "dpe",
                  "lamtot", "st", "utot", "yn"):
            im["st_" + n] = np.asarray(r1.results[c]["st_" + n])
        for n in WMN:
            im["w_m" + n] = WMh[n] if n != "mq0" else np.full(
                (64, 1), 0.0 if c == 0 else 1.0, f)
        for n in W0N:
            im["w_f0" + n] = W0h[n]
        for n in W1N:
            im["w_f1" + n] = W1h[n]
        for n in WRN:
            if n == "mq0x":
                im["w_r" + n] = np.full((128, 1), 0.0 if c == 0 else 1.0, f)
            elif n in ("fm64", "ifm64"):
                fmv = np.zeros((64, 8), f)
                fmv[:, :c] = 1.0
                im["w_r" + n] = fmv if n == "fm64" else 1.0 - fmv
            else:
                im["w_r" + n] = WRh[n]
        im["w_amasku8"] = WRh["amasku8"]
        fmv = np.zeros((64, 8), f)
        fmv[:, :c] = 1.0
        im["sg_ut"] = ut8
        im["sg_lam"] = lam8
        im["sg_fmc"] = fmv
        im["sg_fmc128"] = np.zeros((128, 8), f)
        im["sg_fmc128"][:, :c] = 1.0
        in_maps2.append(im)
    r2 = run_bass_kernel_spmd(_BUILD_CACHE["L2"], in_maps2, core_ids=list(range(NCORES)))
    ut82 = np.stack([np.asarray(r2.results[c]["s2_utot"]) for c in range(NCORES)]).astype(f)
    if "L3" not in _BUILD_CACHE:
        _BUILD_CACHE["L3"] = build_L3()
    in_maps3 = []
    for c in range(NCORES):
        im = dict(conh)
        im["x4"] = np.asarray(r2.results[c]["x4"])
        for n in ("mk", "mv", "mr", "mg", "kchan", "rchan", "kT", "vT", "gg",
                  "st", "utot", "yn"):
            im["s2_" + n] = np.asarray(r2.results[c]["s2_" + n])
        for n in W1N:
            im["w_f1" + n] = W1h[n]
        for n in WRN:
            if n == "mq0x":
                im["w_r" + n] = np.full((128, 1), 0.0 if c == 0 else 1.0, f)
            elif n in ("fm64", "ifm64"):
                fmv = np.zeros((64, 8), f)
                fmv[:, :c] = 1.0
                im["w_r" + n] = fmv if n == "fm64" else 1.0 - fmv
            else:
                im["w_r" + n] = WRh[n]
        im["w_amasku8"] = WRh["amasku8"]
        fmv = np.zeros((64, 8), f)
        fmv[:, :c] = 1.0
        im["g2_ut"] = ut82
        im["g2_fm64"] = fmv
        im["g2_ifm64"] = 1.0 - fmv
        in_maps3.append(im)
    r3 = run_bass_kernel_spmd(_BUILD_CACHE["L3"], in_maps3, core_ids=list(range(NCORES)))
    out = np.concatenate([np.asarray(r3.results[c]["out"]) for c in range(NCORES)],
                         axis=0)
    return out.reshape(1, 16384, D).astype(np.float32)


def build_fused():
    nc = bass.Bass("TRN2", target_bir_lowering=False)
    dri = lambda n, s, d=F32: nc.dram_tensor(n, s, d, kind="Internal").ap()
    x0 = nc.dram_tensor("x0", [TA_, D], F32, kind="ExternalInput").ap()
    out = nc.dram_tensor("out", [T, D], F32, kind="ExternalOutput").ap()
    W0 = _w_decl(nc, {"f0" + n: s for n, s in W0N.items()})
    W0 = {n: W0["f0" + n] for n in W0N}
    WM = _w_decl(nc, {"m" + n: s for n, s in WMN.items()})
    WM = {n: WM["m" + n] for n in WMN}
    W1 = _w_decl(nc, {"f1" + n: s for n, s in W1N.items()})
    W1 = {n: W1["f1" + n] for n in W1N}
    WR = _w_decl(nc, {"r" + n: s for n, s in WRN.items()})
    WR = {n: WR["r" + n] for n in WRN}
    WR["amasku8"] = nc.dram_tensor("w_amasku8", [64, 512], mybir.dt.uint8,
                                   kind="ExternalInput").ap()
    ST = _stash_decls(nc, "Internal")
    ST["sum"] = dri("st_sum", [128, 4104])
    ST2 = _stash2_decls(nc, "Internal")
    agm = dri("ag_m", [8 * 128, 4104])
    agr = dri("ag_r", [8 * 64, 4096])
    x0c = dri("x0c", [DT, 128, pad128(TA_)])
    x1 = dri("x1", [DT, 128, pad128(TA_)])
    x2 = dri("x2", [DT, 128, pad128(QM)])
    x3 = dri("x3", [DT, 128, pad128(QM)])
    x4 = dri("x4", [DT, 128, pad128(QM)])
    x5 = dri("x5", [DT, 128, RW_TP])
    x6 = dri("x6", [DT, 128, RW_TP])
    hgs = dri("hgs", [8, 128, pad128(TA_)], BF16)
    hgs2 = dri("hgs2", [8, 128, pad128(QM)], BF16)
    SG = {"ut": agm.rearrange("(r p) c -> r p c", p=128)[:, :, 0:4096],
          "lam3d": agm.rearrange("(r p) c -> p r c", p=128)[0:64, :, 4096:4104],
          "fmc": nc.dram_tensor("sg_fmc", [64, 8], F32, kind="ExternalInput").ap(),
          "fmc128": nc.dram_tensor("sg_fmc128", [128, 8], F32, kind="ExternalInput").ap()}
    SG2 = {"ut": agr.rearrange("(r p) c -> r p c", p=64),
           "fm64": WR["fm64"], "ifm64": WR["ifm64"]}
    ctx = ExitStack()
    k = Kit(nc, ctx)
    con = Consts(nc, ctx, k)
    ph_input_tokmajor(k, nc, con, x0, TA_, x0c)
    ph_ffn0(k, nc, con, W0, x0c, TA_, x1, "A", hg_hbm=hgs)
    ph_mamba1(k, nc, con, WM, x1, QM, ST)
    k.op("g", "collective_compute", "AllGather", ALU.bypass,
         replica_groups=[list(range(NCORES))], ins=[ST["sum"].opt()],
         outs=[agm.opt()], reads=[ST["sum"]], writes=[agm])
    ph_mamba2(k, nc, con, WM, ST, SG, x1, QM, x2)
    ph_ffn0(k, nc, con, W0, x2, QM, x3, "C", hg_hbm=hgs2)
    ph_ffn1(k, nc, con, W1, x3, QM, x4, "D")
    ph_rwkv1(k, nc, con, WR, x4, ST2)
    k.op("g", "collective_compute", "AllGather", ALU.bypass,
         replica_groups=[list(range(NCORES))], ins=[ST2["utot"].opt()],
         outs=[agr.opt()], reads=[ST2["utot"]], writes=[agr])
    ph_rwkv2(k, nc, con, WR, ST2, SG2, x4, x5)
    ph_ffn1(k, nc, con, W1, x5, RW_TP, x6, "F")
    ph_output_tokmajor(k, nc, con, x6, T, out)
    ctx.close()
    return nc


def kernel_fused(**inputs):
    f = np.float32
    xfull = np.asarray(inputs["L0_attn_hidden_states"], f).reshape(16384, D)
    W0h = host_prep_ffn0(inputs["L0_ffn_A"], inputs["L0_ffn_B"], inputs["L0_ffn_Wr"])
    W1h = host_prep_ffn1(inputs["L1_ffn_weight"], inputs["L1_ffn_weight_scale"],
                         inputs["L1_ffn_bias"])
    conh = Consts.host_inputs()
    WMh = host_prep_mamba(inputs["L0_attn_in_proj_w"], inputs["L0_attn_conv_weight"],
                          inputs["L0_attn_conv_bias"], inputs["L0_attn_dt_bias"],
                          inputs["L0_attn_A_log"], inputs["L0_attn_D"],
                          inputs["L0_attn_norm_weight"], inputs["L0_attn_out_proj_w"], 1)
    WRh = host_prep_rwkv(inputs["L1_attn_w_key"], inputs["L1_attn_w_value"],
                         inputs["L1_attn_w_receptance"], inputs["L1_attn_w_gate"],
                         inputs["L1_attn_w_output"], inputs["L1_attn_time_mix_key"],
                         inputs["L1_attn_time_mix_value"],
                         inputs["L1_attn_time_mix_receptance"],
                         inputs["L1_attn_time_mix_gate"], inputs["L1_attn_time_decay"],
                         inputs["L1_attn_time_faaaa"], inputs["L1_attn_gn_weight"],
                         inputs["L1_attn_gn_bias"], 1)
    if "F" not in _BUILD_CACHE:
        _BUILD_CACHE["F"] = build_fused()
    in_maps = []
    for c in range(NCORES):
        im = dict(conh)
        x0s = np.zeros((TA_, D), f)
        lo = c * T - 4
        src_lo = max(lo, 0)
        x0s[src_lo - lo:, :] = xfull[src_lo:c * T + T]
        im["x0"] = x0s
        for n in W0N:
            im["w_f0" + n] = W0h[n]
        for n in WMN:
            im["w_m" + n] = WMh[n] if n != "mq0" else np.full(
                (64, 1), 0.0 if c == 0 else 1.0, f)
        for n in W1N:
            im["w_f1" + n] = W1h[n]
        fmv = np.zeros((64, 8), f)
        fmv[:, :c] = 1.0
        for n in WRN:
            if n == "mq0x":
                im["w_r" + n] = np.full((128, 1), 0.0 if c == 0 else 1.0, f)
            elif n == "fm64":
                im["w_r" + n] = fmv
            elif n == "ifm64":
                im["w_r" + n] = 1.0 - fmv
            else:
                im["w_r" + n] = WRh[n]
        im["w_amasku8"] = WRh["amasku8"]
        im["sg_fmc"] = fmv
        fmc128 = np.zeros((128, 8), f)
        fmc128[:, :c] = 1.0
        im["sg_fmc128"] = fmc128
        in_maps.append(im)
    r = run_bass_kernel_spmd(_BUILD_CACHE["F"], in_maps, core_ids=list(range(NCORES)))
    outv = np.concatenate([np.asarray(r.results[c]["out"]) for c in range(NCORES)],
                          axis=0)
    return outv.reshape(1, 16384, D).astype(np.float32)


kernel = kernel_fused
